# version 21
# baseline (speedup 1.0000x reference)
import contextlib
import numpy as np
import ml_dtypes

import concourse.bass as bass
import concourse.mybir as mybir
from concourse.bass_utils import run_bass_kernel_spmd

F32 = mybir.dt.float32
BF16 = mybir.dt.bfloat16
F16 = mybir.dt.float16
F8 = mybir.dt.float8e4
AF = mybir.ActivationFunctionType
OP = mybir.AluOpType
AX = mybir.AxisListType

N_CORES = 8
D = 1024
H = 16
TILE = 512
CK = 128
NTC = TILE // CK
HFF = 4096


def _threefry2x32(k0, k1, x0, x1):
    R1 = (13, 15, 26, 6)
    R2 = (17, 29, 16, 24)
    ks0, ks1 = np.uint32(k0), np.uint32(k1)
    ks2 = np.uint32(ks0 ^ ks1 ^ np.uint32(0x1BD11BDA))
    x0 = x0 + ks0
    x1 = x1 + ks1
    ks = (ks1, ks2, ks0)

    def rotl(v, d):
        return (v << np.uint32(d)) | (v >> np.uint32(32 - d))

    for i in range(5):
        for r in R1 if i % 2 == 0 else R2:
            x0 = x0 + x1
            x1 = rotl(x1, r)
            x1 = x1 ^ x0
        x0 = x0 + ks[i % 3]
        x1 = x1 + ks[(i + 1) % 3] + np.uint32(i + 1)
    return x0, x1


def _bernoulli_keep(seed, p, shape):
    size = int(np.prod(shape))
    idx = np.arange(size, dtype=np.uint64)
    hi = (idx >> np.uint64(32)).astype(np.uint32)
    lo = (idx & np.uint64(0xFFFFFFFF)).astype(np.uint32)
    with np.errstate(over="ignore"):
        y0, y1 = _threefry2x32(0, seed, hi, lo)
    bits = y0 ^ y1
    f = ((bits >> np.uint32(9)) | np.uint32(0x3F800000)).view(np.float32) - np.float32(1.0)
    return (f < np.float32(p)).reshape(shape)


class Plan:
    def __init__(self):
        self.steps = []
        self.count = {"ld": 0, "gd": 0, "lw": 0, "gw": 0, "od": 0,
                      "pe": 0, "dv": 0, "sc": 0, "cc": 0}
        self.res = {}

    def emit(self, engine, fn, inc, after=(), touch=()):
        waits = {}
        for r in after:
            for s, v in self.res.get(r, {}).items():
                if waits.get(s, 0) < v:
                    waits[s] = v
        for s in ("ld", "gd", "lw", "gw", "od"):
            if s in waits:
                waits[s] = self.count[s]
        sem, n = inc
        self.count[sem] += n
        self.steps.append((engine, fn, waits, sem))
        done = {sem: self.count[sem]}
        for r in touch:
            self.res[r] = dict(done)

    def mark(self, name, sems):
        self.res[name] = {s: self.count[s] for s in sems}


def build_nc(tokens_per_core, max_steps=None):
    Tc = tokens_per_core
    NT = Tc // TILE
    nc = bass.Bass("TRN2", target_bir_lowering=False)

    x_h = nc.dram_tensor("x", [Tc, D], F32, kind="ExternalInput").ap()
    wq_h = nc.dram_tensor("wq", [D, D], F32, kind="ExternalInput").ap()
    wk_h = nc.dram_tensor("wk", [D, D], F32, kind="ExternalInput").ap()
    wv_h = nc.dram_tensor("wv", [D, D], F32, kind="ExternalInput").ap()
    wo_h = nc.dram_tensor("wo", [D, D], F32, kind="ExternalInput").ap()
    proj2_h = nc.dram_tensor("proj2", [128, 128], F32, kind="ExternalInput").ap()
    w1_h = nc.dram_tensor("w1", [HFF, D], F32, kind="ExternalInput").ap()
    w2_h = nc.dram_tensor("w2", [D, HFF], F32, kind="ExternalInput").ap()
    b1_h = nc.dram_tensor("b1v", [128, 32], F32, kind="ExternalInput").ap()
    b2_h = nc.dram_tensor("b2v", [128, 8], F32, kind="ExternalInput").ap()
    mask_h = nc.dram_tensor("dmask", [D, Tc], F8, kind="ExternalInput").ap()
    tril_h = nc.dram_tensor("trilm", [128, 512], BF16, kind="ExternalInput").ap()
    idf_h = nc.dram_tensor("idf", [128, 128], F32, kind="ExternalInput").ap()
    idb_h = nc.dram_tensor("idb", [128, 128], BF16, kind="ExternalInput").ap()
    i65_h = nc.dram_tensor("i65", [65, 65], BF16, kind="ExternalInput").ap()
    wpref_h = nc.dram_tensor("wpref", [128, N_CORES], F32, kind="ExternalInput").ap()
    out_h = nc.dram_tensor("out", [Tc, D], F32, kind="ExternalOutput").ap()

    zloc_h = nc.dram_tensor("zloc_st", [65, NT * H * TILE], BF16)
    qfst_h = nc.dram_tensor("qf_st", [64, NT * H * TILE], BF16)
    kvloc_h = nc.dram_tensor("kvloc", [64, H * 65], F32)
    kvag_h = nc.dram_tensor("kvag", [N_CORES * 64, H * 65], F32)

    es = contextlib.ExitStack()
    P = Plan()
    with es:
        sbT = lambda name, shape, dt: es.enter_context(nc.sbuf_tensor(name, shape, dt))
        x_sb = sbT("x_sb", [128, 4 * D], F32)
        wo_sb = sbT("wo_sb", [128, 8 * D], BF16)
        proj2_sb = sbT("proj2_sb", [128, 128], BF16)
        idf_sb = sbT("idf_sb", [128, 128], F32)
        idb_sb = sbT("idb_sb", [128, 128], BF16)
        i65_sb = sbT("i65_sb", [65, 65], BF16)
        tril_sb = sbT("tril_sb", [128, 512], BF16)
        wpref_sb = sbT("wpref_sb", [128, N_CORES], F32)
        b1q_sb = sbT("b1q_sb", [128, 32], F32)
        b2q_sb = sbT("b2q_sb", [128, 8], F32)
        bq8_sb = sbT("bq8_sb", [128, 32], F8)
        stat_sb = sbT("stat_sb", [128, 40], F32)
        KV_sb = sbT("KV_sb", [64, H * 65], F32)
        KVc_sb = sbT("KVc_sb", [64, H * 65], BF16)
        KVp_sb = sbT("KVp_sb", [64, H * 65], BF16)
        zloc_sb = sbT("zloc_sb", [65, H * TILE], BF16)
        qf_sb = sbT("qf_sb", [64, H * TILE], BF16)
        kvag_buf = qf_sb[0:64, 0 : 2 * H * 65].bitcast(F32)
        kvpA_sb = qf_sb[0:64, 2 * H * 65 : 4 * H * 65].bitcast(F32)
        kvpB_sb = qf_sb[0:64, 4 * H * 65 : 6 * H * 65].bitcast(F32)

        PS = es.enter_context(nc.psum_tensor("PSf", [128, 3072], F32))
        PSB = es.enter_context(nc.psum_tensor("PSb", [128, 2048], BF16))
        bank = lambda i: PS[:, 512 * i : 512 * (i + 1)]

        ph1 = contextlib.ExitStack()
        s1 = lambda name, shape, dt: ph1.enter_context(nc.sbuf_tensor(name, shape, dt))
        wq_sb = s1("wq_sb", [128, 8 * D], BF16)
        wk_sb = s1("wk_sb", [128, 8 * D], BF16)
        wv_sb = s1("wv_sb", [128, 8 * D], BF16)
        xn_sb = s1("xn_sb", [128, 8 * D], BF16)
        xnf_sb = s1("xnf_sb", [128, 8 * TILE], BF16)
        q1_sb = s1("q1_sb", [128, 2 * TILE], BF16)
        k1_sb = s1("k1_sb", [128, 2 * TILE], BF16)
        kf_sb = s1("kf_sb", [64, H * TILE], BF16)
        ktm_sb = s1("ktm_sb", [128, NTC * 1024], BF16)
        vaug_sb = s1("vaug_sb", [128, NTC * H * 65], BF16)
        am_sb = s1("am_sb", [128, 2 * 512], BF16)
        elu_e = s1("elu_e", [64, 2 * 1024], BF16)
        elu_g = s1("elu_g", [64, 2 * 1024], F32)
        ktm_e = s1("ktm_e", [128, 2 * 512], BF16)
        ktm_g = s1("ktm_g", [128, 2 * 512], F32)
        ph1.close()

        w1t_sb = sbT("w1t_sb", [128, 8 * HFF], F8)
        w2t_sb = sbT("w2t_sb", [128, 32 * D], F8)
        xn2_sb = sbT("xn2_sb", [128, 2 * D], F32)
        xn2q_sb = sbT("xn2q_sb", [128, 8 * TILE], F8)
        h8_sb = sbT("h8_sb", [128, 32 * TILE], F8)
        h16_sb = sbT("h16_sb", [128, 2 * TILE], F16)
        y16_sb = sbT("y16_sb", [128, 2 * TILE], F16)
        t_sb = sbT("t_sb", [128, 8 * TILE], BF16)
        mask_sb = sbT("mask_sb", [128, 8 * TILE], F8)
        zim_sb = sbT("zim_sb", [65, 1024], BF16)
        ztm_sb = sbT("ztm_sb", [128, 4096], BF16)
        zfm_sb = sbT("zfm_sb", [128, 4096], BF16)
        rden_sb = sbT("rden_sb", [128, 16], F32)
        out_sb = xn2_sb

        sem = {}
        for s in ("ld", "gd", "lw", "gw", "od", "pe", "dv", "sc", "cc"):
            sem[s] = es.enter_context(nc.semaphore(s))

        E = P.emit

        wstA = xn_sb[:, 4 * D : 6 * D].bitcast(F32)
        wstB = xn_sb[:, 6 * D : 8 * D].bitcast(F32)

        for dst, src, r in (
            (tril_sb[:, :], tril_h, "tril"),
            (idf_sb[:, :], idf_h, "idf"),
            (idb_sb[:, :], idb_h, "idb"),
            (i65_sb[:, :], i65_h, "i65"),
            (wpref_sb[:, :], wpref_h, "wpref"),
            (b1q_sb[:, :], b1_h, "b1raw"),
            (b2q_sb[:, :], b2_h, "b2raw"),
        ):
            E("sync", (lambda d, s: lambda e: e.dma_start(out=d, in_=s))(dst, src),
              ("ld", 16), touch=(r,))

        E("sync", lambda e: e.dma_start(out=wstA[:, 0:128], in_=proj2_h),
          ("lw", 16), touch=("pstage",))
        E("vector", lambda e: e.tensor_copy(proj2_sb[:, :], wstA[:, 0:128]),
          ("dv", 1), after=("pstage",), touch=("proj2", "wst0_c"))
        E("vector", lambda e: e.memset(stat_sb[:, 6:7], 1e-5), ("dv", 1), touch=("epsc",))
        E("vector", lambda e: e.memset(vaug_sb[:, 64 : NTC * H * 65 : 65], 1.0),
          ("dv", 1), touch=("vaug_ones",))

        def ln_stats_and_norm(tc, out_tile, out_res, src_res, extra=()):
            pp = tc % 2
            xs = x_sb[:, tc * D : (tc + 1) * D]
            c6 = stat_sb[:, 16 + 12 * pp : 28 + 12 * pp]
            b0 = pp * 8
            c2 = stat_sb[:, b0 : b0 + 2]
            c_nm = stat_sb[:, b0 + 2 : b0 + 3]
            c_t1 = stat_sb[:, b0 + 4 : b0 + 5]
            c_r = stat_sb[:, b0 + 5 : b0 + 6]
            E("vector",
              (lambda xs_, c6_: lambda e: e.bn_stats(c6_[:, 0:6], xs_[:, 0:512]))(xs, c6),
              ("dv", 1), after=(src_res, f"stats_used{pp}"), touch=(f"st_6{pp}",))
            E("vector",
              (lambda xs_, c6_: lambda e: e.bn_stats(c6_[:, 6:12], xs_[:, 512:1024]))(xs, c6),
              ("dv", 1), after=(src_res, f"st_6{pp}"), touch=(f"st_6b{pp}",))
            E("vector", (lambda c2_, c6_: lambda e: e.bn_aggr(c2_, c6_))(c2, c6),
              ("dv", 1), after=(f"st_6{pp}", f"st_6b{pp}"), touch=(f"st_2{pp}",))
            E("vector",
              (lambda c_nm_, c2_: lambda e: e.tensor_scalar_mul(c_nm_, c2_[:, 0:1], -1.0))(c_nm, c2),
              ("dv", 1), after=(f"st_2{pp}",), touch=(f"st_nm{pp}",))
            E("scalar",
              (lambda c_t1_, c2_: lambda e: e.activation(
                  c_t1_, c2_[:, 1:2], AF.Ln, bias=stat_sb[:, 6:7]))(c_t1, c2),
              ("sc", 1), after=(f"st_2{pp}", "epsc"), touch=(f"st_ln{pp}",))
            E("scalar",
              (lambda c_r_, c_t1_: lambda e: e.activation(c_r_, c_t1_, AF.Exp, scale=-0.5))(c_r, c_t1),
              ("sc", 1), after=(f"st_ln{pp}",), touch=(f"st_r{pp}",))
            E("vector",
              (lambda xs_, ot, c_nm_, c_r_: lambda e: e.tensor_scalar(
                  ot, xs_, c_nm_, c_r_, OP.add, OP.mult))(xs, out_tile, c_nm, c_r),
              ("dv", 1), after=(f"st_r{pp}", f"st_nm{pp}", out_res + "_used") + extra,
              touch=(out_res, f"stats_used{pp}"))

        def emit_xln(t):
            tok0 = t * TILE
            xnsl = (t % 2) * 4 * D
            for tc in range(NTC):
                E("sync",
                  (lambda tc_, tok0_: lambda e: e.dma_start(
                      out=x_sb[:, tc_ * D : (tc_ + 1) * D],
                      in_=x_h[tok0_ + tc_ * 128 : tok0_ + (tc_ + 1) * 128, :]))(tc, tok0),
                  ("ld", 16), after=("x_used",), touch=(f"x_{tc}",))
            for tc in range(NTC):
                ln_stats_and_norm(tc, xn_sb[:, xnsl + tc * D : xnsl + (tc + 1) * D],
                                  f"xn{t % 2}_{tc}", f"x_{tc}",
                                  extra=("w3",) if t == 1 else ())
            P.mark("x_used", ("dv", "sc"))

        emit_xln(0)

        E("vector", lambda e: e.tensor_copy(bq8_sb[:, 0:32], b1q_sb[:, :]),
          ("dv", 1), after=("b1raw",), touch=("b8",))
        E("vector", lambda e: e.tensor_copy(b1q_sb[:, :], bq8_sb[:, 0:32]),
          ("dv", 1), after=("b8",), touch=("b1q", "b8f"))
        E("vector", lambda e: e.tensor_copy(bq8_sb[:, 0:8], b2q_sb[:, :]),
          ("dv", 1), after=("b2raw", "b8f"), touch=("b82",))
        E("vector", lambda e: e.tensor_copy(b2q_sb[:, :], bq8_sb[:, 0:8]),
          ("dv", 1), after=("b82",), touch=("b2q",))

        whs = (wq_h, wk_h, wv_h, wo_h)
        wsbs = (wq_sb, wk_sb, wv_sb, wo_sb)

        def emit_wpair(wi, dc):
            p = dc % 2
            stg = wstA if p == 0 else wstB
            E("sync",
              (lambda wh_, dc_, stg_: lambda e: e.dma_start(
                  out=stg_[:, :], in_=wh_[dc_ * 128 : (dc_ + 1) * 128, :]))(whs[wi], dc, stg),
              ("lw" if p == 0 else "gw", 16), after=(f"wst{p}_c",), touch=(f"wst{p}",))
            fin = ("w3",) if wi == 3 and dc == 7 else ()
            if wi in (0, 1):
                E("vector",
                  (lambda wsb_, dc_, stg_: lambda e: e.tensor_copy(
                      wsb_[:, dc_ * D : (dc_ + 1) * D], stg_[:, :]))(wsbs[wi], dc, stg),
                  ("dv", 1), after=(f"wst{p}",), touch=(f"wst{p}_c", f"w{wi}_{dc}") + fin)
            else:
                E("scalar",
                  (lambda wsb_, dc_, stg_: lambda e: e.copy(
                      wsb_[:, dc_ * D : (dc_ + 1) * D], stg_[:, :]))(wsbs[wi], dc, stg),
                  ("sc", 1), after=(f"wst{p}",), touch=(f"wst{p}_c", f"w{wi}_{dc}") + fin)

        for dc in range(8):
            emit_wpair(0, dc)
        for dc in range(8):
            emit_wpair(1, dc)
        wpair_at_oc = {2: [(2, 0), (2, 1)], 3: [(2, 2), (2, 3)],
                       4: [(2, 4), (2, 5)], 5: [(2, 6), (2, 7)],
                       6: [(3, 0), (3, 1), (3, 2), (3, 3)],
                       7: [(3, 4), (3, 5), (3, 6), (3, 7)]}
        for t in range(NT):
            xnsl = (t % 2) * 4 * D
            for dc in range(8):
                for tc in range(NTC):
                    E("tensor",
                      (lambda dc_, tc_, xnsl_: lambda e: e.transpose(
                          PSB[:, tc_ * 128 : (tc_ + 1) * 128],
                          xn_sb[:, xnsl_ + tc_ * D + dc_ * 128 : xnsl_ + tc_ * D + (dc_ + 1) * 128],
                          idb_sb[:, :]))(dc, tc, xnsl),
                      ("pe", 1), after=(f"xn{t % 2}_{tc}", "psb6_free", "idb"), touch=("psb6",))
                E("scalar",
                  (lambda dc_: lambda e: e.copy(xnf_sb[:, dc_ * TILE : (dc_ + 1) * TILE], PSB[:, 0:512]))(dc),
                  ("sc", 1), after=("psb6",), touch=(f"xnf_{dc}", "psb6_free"))
            for tc in range(NTC):
                P.mark(f"xn{t % 2}_{tc}_used", ("pe",))
            def emit_proj(oc):
                sl = oc % 2
                q1v = q1_sb[:, sl * TILE : (sl + 1) * TILE]
                k1v = k1_sb[:, sl * TILE : (sl + 1) * TILE]
                bq = 5 if oc % 2 == 0 else 3
                bk = 3 if oc % 2 == 0 else 5
                for dc in range(8):
                    E("tensor",
                      (lambda oc_, dc_, bq_: lambda e: e.matmul(
                          bank(bq_),
                          wq_sb[:, dc_ * D + oc_ * 128 : dc_ * D + (oc_ + 1) * 128],
                          xnf_sb[:, dc_ * TILE : (dc_ + 1) * TILE],
                          start=(dc_ == 0), stop=(dc_ == 7)))(oc, dc, bq),
                      ("pe", 1), after=(f"xnf_{dc}", f"b{bq}_free", f"w0_{dc}"), touch=(f"b{bq}",))
                E("scalar", (lambda bq_, q1v_: lambda e: e.copy(q1v_, bank(bq_)))(bq, q1v),
                  ("sc", 1), after=(f"b{bq}", f"q1_used{sl}"), touch=(f"q1_{sl}", f"b{bq}_free"))
                for dc in range(8):
                    E("tensor",
                      (lambda oc_, dc_, bk_: lambda e: e.matmul(
                          bank(bk_),
                          wk_sb[:, dc_ * D + oc_ * 128 : dc_ * D + (oc_ + 1) * 128],
                          xnf_sb[:, dc_ * TILE : (dc_ + 1) * TILE],
                          start=(dc_ == 0), stop=(dc_ == 7)))(oc, dc, bk),
                      ("pe", 1), after=(f"xnf_{dc}", f"b{bk}_free", f"w1_{dc}"), touch=(f"b{bk}",))
                E("scalar", (lambda bk_, k1v_: lambda e: e.copy(k1v_, bank(bk_)))(bk, k1v),
                  ("sc", 1), after=(f"b{bk}", f"k1_used{sl}"), touch=(f"k1_{sl}", f"b{bk}_free"))

            def emit_fmktm(oc):
                sl = oc % 2
                q1v = q1_sb[:, sl * TILE : (sl + 1) * TILE]
                k1v = k1_sb[:, sl * TILE : (sl + 1) * TILE]
                for which, src_res, src_sb, dst_sb, dst_res, slot in (
                    ("q", f"q1_{sl}", q1v, qf_sb, "qf", 0),
                    ("k", f"k1_{sl}", k1v, kf_sb, "kf", 1),
                ):
                    fb = 0 if (oc + slot) % 2 == 0 else 2
                    pf = PS[0:64, fb * 512 : fb * 512 + 1024]
                    ee = elu_e[:, slot * 1024 : (slot + 1) * 1024]
                    gg = elu_g[:, slot * 1024 : (slot + 1) * 1024]
                    for half in range(2):
                        E("tensor",
                          (lambda half_, fb_, src_: lambda e: e.matmul(
                              PS[0:64, (fb_ + half_) * 512 : (fb_ + half_) * 512 + 512],
                              proj2_sb[:, half_ * 64 : half_ * 64 + 64],
                              src_[:, :], start=True, stop=True))(half, fb, src_sb),
                          ("pe", 1),
                          after=(src_res, f"b{fb + half}_free", "proj2"),
                          touch=(f"b{fb + half}",) + ((f"q1_used{sl}",) if which == "q" and half == 1 else ()))
                    E("scalar",
                      (lambda pf_, ee_: lambda e: e.activation(ee_, pf_, AF.Exp))(pf, ee),
                      ("sc", 1), after=(f"b{fb}", f"b{fb + 1}", f"elu_e_used{slot}"),
                      touch=(f"elu_e{slot}",))
                    E("scalar",
                      (lambda pf_, gg_: lambda e: e.activation(gg_, pf_, AF.Relu))(pf, gg),
                      ("sc", 1), after=(f"b{fb}", f"b{fb + 1}", f"elu_g_used{slot}"),
                      touch=(f"elu_g{slot}", f"b{fb}_free", f"b{fb + 1}_free"))
                    E("vector",
                      (lambda oc_, ee_, gg_, dst_: lambda e: e.scalar_tensor_tensor(
                          dst_[0:64, oc_ * 1024 : (oc_ + 1) * 1024], ee_, 1.0, gg_,
                          OP.min, OP.add))(oc, ee, gg, dst_sb),
                      ("dv", 1), after=(f"elu_e{slot}", f"elu_g{slot}")
                      + ((f"qf_used2_h{oc // 4}",) if which == "q" else ()),
                      touch=(dst_res, f"elu_e_used{slot}", f"elu_g_used{slot}")
                      + ((f"qfh{oc // 4}",) if which == "q" and oc % 4 == 3 else ()))
                for tc in range(NTC):
                    E("tensor",
                      (lambda tc_, k1v_: lambda e: e.matmul(
                          PS[:, 2048 + tc_ * 128 : 2048 + (tc_ + 1) * 128],
                          k1v_[:, tc_ * 128 : (tc_ + 1) * 128],
                          proj2_sb[:, :], start=True, stop=True))(tc, k1v),
                      ("pe", 1), after=(f"k1_{sl}", "b4_free", "proj2"),
                      touch=("b4",) + ((f"k1_used{sl}",) if tc == NTC - 1 else ()))
                ks_ = oc % 2
                kte = ktm_e[:, ks_ * 512 : (ks_ + 1) * 512]
                ktg = ktm_g[:, ks_ * 512 : (ks_ + 1) * 512]
                E("scalar", (lambda kte_: lambda e: e.activation(kte_, bank(4), AF.Exp))(kte),
                  ("sc", 1), after=("b4", f"ktm_e_used{ks_}"), touch=(f"ktm_e{ks_}",))
                E("scalar", (lambda ktg_: lambda e: e.activation(ktg_, bank(4), AF.Relu))(ktg),
                  ("sc", 1), after=("b4", f"ktm_g_used{ks_}"),
                  touch=(f"ktm_g{ks_}", "b4_free"))
                E("vector",
                  (lambda oc_, kte_, ktg_: lambda e: e.scalar_tensor_tensor(
                      ktm_sb[:, :].rearrange("p (t c) -> p t c", t=NTC)[:, :, oc_ * 128 : (oc_ + 1) * 128],
                      kte_.rearrange("p (t c) -> p t c", t=NTC), 1.0,
                      ktg_.rearrange("p (t c) -> p t c", t=NTC),
                      OP.min, OP.add))(oc, kte, ktg),
                  ("dv", 1), after=(f"ktm_e{ks_}", f"ktm_g{ks_}"),
                  touch=("ktm", f"ktm_e_used{ks_}", f"ktm_g_used{ks_}"))
                if oc in (3, 7):
                    hh = oc // 4
                    E("gpsimd",
                      (lambda t_, hh_: lambda e: e.dma_start(
                          out=qfst_h[:, t_ * H * TILE + hh_ * 4096
                                     : t_ * H * TILE + (hh_ + 1) * 4096],
                          in_=qf_sb[0:64, hh_ * 4096 : (hh_ + 1) * 4096]))(t, hh),
                      ("gd", 16), after=(f"qfh{hh}",),
                      touch=(f"qf_used2_h{hh}",)
                      + (("qf_stashed", "qf_used2") if hh == 1 else ()))

            def emit_kvc(tc):
                if t == 0 and tc == 0:
                    E("vector", lambda e: e.memset(KV_sb[:, :], 0.0),
                      ("dv", 1), after=("kv_used",), touch=("kv_state",))
                E("vector", lambda e: e.tensor_copy(KVc_sb[:, :], KV_sb[:, :]),
                  ("dv", 1), after=("kv_state", "kvc_used"), touch=("kvc",))

            def emit_A(g, tc):
                ab = 4 if g % 2 == 0 else 2
                asl = g % 2
                amv = am_sb[:, asl * 512 : (asl + 1) * 512]
                for hl in range(4):
                    h = g * 4 + hl
                    E("tensor",
                      (lambda h_, hl_, tc_, ab_: lambda e: e.matmul(
                          PS[:, ab_ * 512 + hl_ * 128 : ab_ * 512 + (hl_ + 1) * 128],
                          kf_sb[0:64, h_ * TILE + tc_ * 128 : h_ * TILE + (tc_ + 1) * 128],
                          qf_sb[0:64, h_ * TILE + tc_ * 128 : h_ * TILE + (tc_ + 1) * 128],
                          start=True, stop=True))(h, hl, tc, ab),
                      ("pe", 1), after=("kf", "qf", f"b{ab}_free"), touch=(f"b{ab}",))
                E("vector",
                  (lambda ab_, amv_: lambda e: e.tensor_tensor(amv_, bank(ab_), tril_sb[:, :], OP.mult))(ab, amv),
                  ("dv", 1), after=(f"b{ab}", "tril", f"am_used{asl}"),
                  touch=(f"am{asl}", f"b{ab}_free"))

            def emit_z(g, tc):
                zb = 5 if g % 2 == 0 else 3
                asl = g % 2
                amv = am_sb[:, asl * 512 : (asl + 1) * 512]
                for hl in range(4):
                    h = g * 4 + hl
                    E("tensor",
                      (lambda h_, hl_, tc_, zb_, amv_: lambda e: e.matmul(
                          PS[0:65, zb_ * 512 + hl_ * 128 : zb_ * 512 + (hl_ + 1) * 128],
                          vaug_sb[:, tc_ * H * 65 + h_ * 65 : tc_ * H * 65 + (h_ + 1) * 65],
                          amv_[:, hl_ * 128 : (hl_ + 1) * 128],
                          start=(hl_ == 0), stop=False))(h, hl, tc, zb, amv),
                      ("pe", 1),
                      after=(f"am{asl}", f"vaug_{tc}_{g // 2}", f"b{zb}_free", "kvc", "qf"),
                      touch=(f"b{zb}z",))
                    E("tensor",
                      (lambda h_, hl_, tc_, zb_: lambda e: e.matmul(
                          PS[0:65, zb_ * 512 + hl_ * 128 : zb_ * 512 + (hl_ + 1) * 128],
                          KVc_sb[:, h_ * 65 : (h_ + 1) * 65],
                          qf_sb[0:64, h_ * TILE + tc_ * 128 : h_ * TILE + (tc_ + 1) * 128],
                          start=False, stop=(hl_ == 3)))(h, hl, tc, zb),
                      ("pe", 1), touch=(f"b{zb}z",))
                E("scalar",
                  (lambda tc_, g_, zb_: lambda e: e.copy(
                      zloc_sb[0:65, (tc_ * 4 + g_) * 512 : (tc_ * 4 + g_ + 1) * 512],
                      PS[0:65, zb_ * 512 : zb_ * 512 + 512]))(tc, g, zb),
                  ("sc", 1), after=(f"b{zb}z", f"zloc_used_tc{tc}"),
                  touch=("zloc", f"b{zb}_free", f"am_used{asl}")
                  + ((f"zloc_tc{tc}",) if g == 3 else ()))

            def emit_kv(g4, tc):
                for hh in range(4):
                    h = g4 * 4 + hh
                    E("tensor",
                      (lambda h_, tc_, g4_: lambda e: e.matmul(
                          PS[0:64, g4_ * 512 + (h_ % 4) * 65 : g4_ * 512 + (h_ % 4) * 65 + 65],
                          ktm_sb[:, tc_ * 1024 + h_ * 64 : tc_ * 1024 + (h_ + 1) * 64],
                          vaug_sb[:, tc_ * H * 65 + h_ * 65 : tc_ * H * 65 + (h_ + 1) * 65],
                          start=(h_ % 4 == 0),
                          stop=(h_ % 4 == 3)))(h, tc, g4),
                      ("pe", 1),
                      after=("ktm", f"vaug_{tc}_{g4 // 2}", "kvc", f"b{g4}_free", "fold_done"),
                      touch=(f"pk_{g4}", f"b{g4}"))

            for oc in range(8):
                emit_proj(oc)
                if oc >= 1:
                    emit_fmktm(oc - 1)
                if t == 0:
                    for wi, dc in wpair_at_oc.get(oc, ()):
                        emit_wpair(wi, dc)
            emit_fmktm(7)
            emit_kvc(0)
            emit_A(0, 0)
            for tc in range(NTC):
                for half in range(2):
                    vb = 5 if (tc * 2 + half) % 2 == 0 else 3
                    for dc in range(8):
                        E("tensor",
                          (lambda tc_, half_, dc_, vb_: lambda e: e.matmul(
                              bank(vb_),
                              xnf_sb[:, dc_ * TILE + tc_ * 128 : dc_ * TILE + (tc_ + 1) * 128],
                              wv_sb[:, dc_ * D + half_ * 512 : dc_ * D + half_ * 512 + 512],
                              start=(dc_ == 0), stop=(dc_ == 7)))(tc, half, dc, vb),
                          ("pe", 1), after=(f"xnf_{dc}", f"b{vb}_free", f"w2_{dc}"), touch=(f"b{vb}",))
                    E("vector",
                      (lambda tc_, half_, vb_: lambda e: e.tensor_copy(
                          vaug_sb[:, :].rearrange("p (t h c) -> p t h c", t=NTC, h=H)[
                              :, tc_, half_ * 8 : half_ * 8 + 8, 0:64],
                          bank(vb_).rearrange("p (h c) -> p h c", h=8)))(tc, half, vb),
                      ("dv", 1), after=(f"b{vb}", "vaug_ones", "vaug_used"),
                      touch=("vaug", f"vaug_{tc}_{half}", f"b{vb}_free"))
            if t + 1 < NT:
                emit_xln(t + 1)
            for tc in range(NTC):
                if tc > 0:
                    emit_kvc(tc)
                if tc > 0:
                    emit_A(0, tc)
                for g in range(4):
                    if g + 1 < 4:
                        emit_A(g + 1, tc)
                    emit_z(g, tc)
                    emit_kv(g, tc)
                for g4 in range(4):
                    E("vector",
                      (lambda g4_: lambda e: e.tensor_tensor(
                          KV_sb[:, g4_ * 260 : (g4_ + 1) * 260],
                          PS[0:64, g4_ * 512 : g4_ * 512 + 260],
                          KV_sb[:, g4_ * 260 : (g4_ + 1) * 260], OP.add))(g4),
                      ("dv", 1), after=(f"pk_{g4}", "kv_state"),
                      touch=("kv_state", "kvc_used", "fold_done", f"b{g4}_free"))
                E("gpsimd",
                  (lambda t_, tc_: lambda e: e.dma_start(
                      out=zloc_h[:, t_ * H * TILE + tc_ * 2048
                                 : t_ * H * TILE + (tc_ + 1) * 2048],
                      in_=zloc_sb[:, tc_ * 2048 : (tc_ + 1) * 2048]))(t, tc),
                  ("gd", 16), after=(f"zloc_tc{tc}",),
                  touch=(f"zloc_used_tc{tc}", "zloc_stashed"))
            P.mark("vaug_used", ("pe",))

        P.mark("ph1_pe_done", ("pe",))
        h8f32 = h8_sb[:, :].bitcast(F32)
        stgs = (x_sb, h8f32)
        for hg in range(8):
            p = hg % 2
            stg = stgs[p]
            for j in range(4):
                E("sync",
                  (lambda hg_, j_, stg_: lambda e: e.dma_start(
                      out=stg_[:, j_ * D : (j_ + 1) * D],
                      in_=w1_h[hg_ * 512 + j_ * 128 : hg_ * 512 + (j_ + 1) * 128, :]))(hg, j, stg),
                  ("gw" if p else "lw", 16),
                  after=(f"w1stage_used_{p}", "x_used") + (("ph1_pe_done",) if p else ()),
                  touch=(f"w1st_{p}_{j}",))
            for dc in range(8):
                pbk = 4 if dc % 2 == 0 else 2
                for j in range(4):
                    E("tensor",
                      (lambda dc_, j_, pbk_, stg_: lambda e: e.transpose(
                          PS[:, pbk_ * 512 + j_ * 128 : pbk_ * 512 + (j_ + 1) * 128],
                          stg_[:, j_ * D + dc_ * 128 : j_ * D + (dc_ + 1) * 128],
                          idf_sb[:, :]))(dc, j, pbk, stg),
                      ("pe", 1), after=(f"w1st_{p}_{j}", f"b{pbk}_free", "idf"), touch=(f"b{pbk}",))
                E("scalar",
                  (lambda hg_, dc_, pbk_: lambda e: e.copy(
                      w1t_sb[:, dc_ * HFF + hg_ * 512 : dc_ * HFF + (hg_ + 1) * 512],
                      bank(pbk_)))(hg, dc, pbk),
                  ("sc", 1), after=(f"b{pbk}",),
                  touch=("w1t", f"b{pbk}_free") + ((f"w1stage_used_{p}",) if dc == 7 else ()))
        for og in range(8):
            p = og % 2
            stg = stgs[p]
            E("sync",
              (lambda og_, stg_: lambda e: e.dma_start(
                  out=stg_[:, :], in_=w2_h[og_ * 128 : (og_ + 1) * 128, :]))(og, stg),
              ("gw" if p else "lw", 16),
              after=(f"w2stage_used_{p}", f"w1stage_used_{p}"), touch=(f"w2st_{p}",))
            for hcg in range(8):
                pbk = 4 if hcg % 2 == 0 else 2
                for j in range(4):
                    E("tensor",
                      (lambda hcg_, j_, pbk_, stg_: lambda e: e.transpose(
                          PS[:, pbk_ * 512 + j_ * 128 : pbk_ * 512 + (j_ + 1) * 128],
                          stg_[:, (hcg_ * 4 + j_) * 128 : (hcg_ * 4 + j_ + 1) * 128],
                          idf_sb[:, :]))(hcg, j, pbk, stg),
                      ("pe", 1), after=(f"w2st_{p}", f"b{pbk}_free", "idf"), touch=(f"b{pbk}",))
                E("scalar",
                  (lambda og_, hcg_, pbk_: lambda e: e.copy(
                      w2t_sb[:, :].rearrange("p (hc c) -> p hc c", hc=32)[
                          :, hcg_ * 4 : hcg_ * 4 + 4, og_ * 128 : (og_ + 1) * 128],
                      bank(pbk_).rearrange("p (j c) -> p j c", j=4)))(og, hcg, pbk),
                  ("sc", 1), after=(f"b{pbk}",),
                  touch=("w2t", f"b{pbk}_free") + ((f"w2stage_used_{p}",) if hcg == 7 else ()))

        E("gpsimd", lambda e: e.dma_start(out=kvloc_h[:, :], in_=KV_sb[:, :]),
          ("gd", 16), after=("kv_state",), touch=("kvloc_d", "kv_used"))
        E("gpsimd",
          lambda e: e.collective_compute(
              "AllGather", OP.bypass, replica_groups=[list(range(N_CORES))],
              ins=[kvloc_h.ap().opt()], outs=[kvag_h.ap().opt()]),
          ("cc", 1), after=("kvloc_d",), touch=("kvag_d",))
        E("vector", lambda e: e.memset(kvpA_sb[:, :], 0.0), ("dv", 1), after=("qf_stashed",), touch=("kvp0",))
        for j in range(N_CORES):
            src, dst = (kvpA_sb, kvpB_sb) if j % 2 == 0 else (kvpB_sb, kvpA_sb)
            E("gpsimd",
              (lambda j_: lambda e: e.dma_start(
                  out=kvag_buf[:, :], in_=kvag_h.ap()[j_ * 64 : (j_ + 1) * 64, :]))(j),
              ("gd", 16), after=("kvag_d", f"kvagb_used", "qf_stashed"), touch=("kvagb",))
            E("vector",
              (lambda j_, s_, d_: lambda e: e.scalar_tensor_tensor(
                  d_[:, :], kvag_buf[:, :],
                  wpref_sb[0:64, j_ : j_ + 1], s_[:, :], OP.mult, OP.add))(j, src, dst),
              ("dv", 1), after=("kvagb", "wpref", f"kvp{j}"), touch=(f"kvp{j + 1}", "kvagb_used"))
        kvp_fin = kvpA_sb if N_CORES % 2 == 0 else kvpB_sb
        E("vector", (lambda kf_: lambda e: e.tensor_copy(KVp_sb[:, :], kf_[:, :]))(kvp_fin),
          ("dv", 1), after=(f"kvp{N_CORES}",), touch=("kvp",))

        def emit_p2_gdloads(t):
            E("gpsimd",
              (lambda t_: lambda e: e.dma_start(
                  out=zloc_sb[:, :], in_=zloc_h[:, t_ * H * TILE : (t_ + 1) * H * TILE]))(t),
              ("gd", 16), after=("zloc_used2", "zloc_stashed", "zloc"), touch=("zloc2",))
            E("gpsimd",
              (lambda t_: lambda e: e.dma_start(
                  out=qf_sb[0:64, :], in_=qfst_h[:, t_ * H * TILE : (t_ + 1) * H * TILE]))(t),
              ("gd", 16), after=("qf_used2", "qf_stashed", "kvp"), touch=("qf2",))

        def emit_p2_mask(t):
            E("gpsimd",
              (lambda t_: lambda e: e.dma_start(
                  out=mask_sb[:, :].rearrange("p (o c) -> p o c", o=8),
                  in_=mask_h.rearrange("(o p) tt -> p o tt", p=128)[
                      :, :, t_ * TILE : (t_ + 1) * TILE]))(t),
              ("od", 16), after=("mask_used",), touch=("mask",))

        emit_p2_gdloads(0)
        emit_p2_mask(0)
        for t in range(NT):
            tok0 = t * TILE
            for tc in range(NTC):
                E("sync",
                  (lambda tc_, tok0_: lambda e: e.dma_start(
                      out=x_sb[:, tc_ * D : (tc_ + 1) * D],
                      in_=x_h[tok0_ + tc_ * 128 : tok0_ + (tc_ + 1) * 128, :]))(tc, tok0),
                  ("ld", 16), after=("p2x_used", "w1stage_used_0", "w2stage_used_0"),
                  touch=(f"p2x_{tc}",))
            def emit_ztot(gi):
                tc, g = gi // 4, gi % 4
                zb = 5 if gi % 2 == 0 else 0
                E("tensor",
                  (lambda gi_, zb_: lambda e: e.matmul(
                      PS[0:65, zb_ * 512 : zb_ * 512 + 512], i65_sb[:, :],
                      zloc_sb[0:65, gi_ * 512 : (gi_ + 1) * 512],
                      start=True, stop=False))(gi, zb),
                  ("pe", 1), after=("zloc2", "i65", f"b{zb}_free", "kvp", "qf2"),
                  touch=(f"b{zb}z",))
                for hl in range(4):
                    h = g * 4 + hl
                    E("tensor",
                      (lambda h_, hl_, tc_, zb_: lambda e: e.matmul(
                          PS[0:65, zb_ * 512 + hl_ * 128 : zb_ * 512 + (hl_ + 1) * 128],
                          KVp_sb[:, h_ * 65 : (h_ + 1) * 65],
                          qf_sb[0:64, h_ * TILE + tc_ * 128 : h_ * TILE + (tc_ + 1) * 128],
                          start=False, stop=(hl_ == 3)))(h, hl, tc, zb),
                      ("pe", 1), touch=(f"b{zb}z",))

            def emit_zim(gi):
                zb = 5 if gi % 2 == 0 else 0
                zs = gi % 2
                zimv = zim_sb[:, zs * 512 : (zs + 1) * 512]
                E("scalar",
                  (lambda zb_, zimv_: lambda e: e.copy(zimv_, PS[0:65, zb_ * 512 : zb_ * 512 + 512]))(zb, zimv),
                  ("sc", 1), after=(f"b{zb}z", f"zim_used{zs}"),
                  touch=(f"zim{zs}", f"b{zb}_free"))

            def emit_div(gi):
                tc, g = gi // 4, gi % 4
                zs = gi % 2
                zimv = zim_sb[:, zs * 512 : (zs + 1) * 512]
                t1c = 1024
                for hl in range(4):
                    E("tensor",
                      (lambda hl_, zimv_, t1c_: lambda e: e.transpose(
                          PSB[:, t1c_ + hl_ * 66 : t1c_ + hl_ * 66 + 65],
                          zimv_[0:65, hl_ * 128 : (hl_ + 1) * 128],
                          i65_sb[:, :]))(hl, zimv, t1c),
                      ("pe", 1), after=(f"zim{zs}", "psb70_free", "i65"),
                      touch=("psb70",))
                E("vector",
                  (lambda zs_, t1c_: lambda e: e.reciprocal(
                      rden_sb[:, zs_ * 4 : (zs_ + 1) * 4],
                      PSB[:, t1c_ + 64 : t1c_ + 64 + 3 * 66 + 1 : 66]))(zs, t1c),
                  ("dv", 1), after=("psb70",), touch=(f"rden{zs}",))
                base = tc * 1024 + g * 256
                t1v = PSB[:, t1c : t1c + 264].rearrange("p (h c) -> p h c", h=4)[:, :, 0:64]
                rdv = rden_sb[:, zs * 4 : zs * 4 + 4].rearrange("p (h c) -> p h c", h=4)
                t1b, rdb = bass.broadcast_tensor_aps(t1v, rdv)
                E("vector",
                  (lambda t1b_, rdb_, base_: lambda e: e.tensor_tensor(
                      ztm_sb[:, base_ : base_ + 256].rearrange("p (h c) -> p h c", h=4),
                      t1b_, rdb_, OP.mult))(t1b, rdb, base),
                  ("dv", 1), after=(f"rden{zs}", "psb70", f"ztm_used{tc}"),
                  touch=(f"ztm{tc}", "psb70_free", f"zim_used{zs}"))

            emit_ztot(0)
            emit_ztot(1)
            for gi in range(16):
                emit_zim(gi)
                if gi + 2 < 16:
                    emit_ztot(gi + 2)
                emit_div(gi)
            P.mark("zloc_used2", ("pe",))
            P.mark("qf_used2", ("pe",))
            if t + 1 < NT:
                emit_p2_gdloads(t + 1)

            def emit_t2(tc):
                for p in range(8):
                    E("tensor",
                      (lambda p_, tc_: lambda e: e.transpose(
                          PSB[:, p_ * 128 : (p_ + 1) * 128],
                          ztm_sb[:, tc_ * 1024 + p_ * 128 : tc_ * 1024 + (p_ + 1) * 128],
                          idb_sb[:, :]))(p, tc),
                      ("pe", 1), after=(f"ztm{tc}", "psb6_free", "idb"), touch=("psb6",))
                E("scalar",
                  (lambda tc_: lambda e: e.copy(zfm_sb[:, tc_ * 1024 : (tc_ + 1) * 1024], PSB[:, 0:1024]))(tc),
                  ("sc", 1), after=("psb6",), touch=(f"zfm{tc}", "psb6_free", f"ztm_used{tc}"))

            def emit_wo(tc):
                for half in range(2):
                    for p in range(8):
                        E("tensor",
                          (lambda half_, p_, tc_: lambda e: e.matmul(
                              PS[:, 1536 + half_ * 512 : 1536 + half_ * 512 + 512],
                              zfm_sb[:, tc_ * 1024 + p_ * 128 : tc_ * 1024 + (p_ + 1) * 128],
                              wo_sb[:, p_ * D + half_ * 512 : p_ * D + half_ * 512 + 512],
                              start=(p_ == 0), stop=(p_ == 7)))(half, p, tc),
                          ("pe", 1), after=(f"zfm{tc}", "w3", f"b{3 + half}_free"),
                          touch=(f"b{3 + half}",))
                    E("vector",
                      (lambda tc_, half_: lambda e: e.tensor_tensor(
                          x_sb[:, tc_ * D + half_ * 512 : tc_ * D + half_ * 512 + 512],
                          PS[:, 1536 + half_ * 512 : 1536 + half_ * 512 + 512],
                          x_sb[:, tc_ * D + half_ * 512 : tc_ * D + half_ * 512 + 512],
                          OP.add))(tc, half),
                      ("dv", 1), after=(f"b{3 + half}", f"p2x_{tc}"),
                      touch=(f"x1_{tc}", f"b{3 + half}_free"))

            emit_t2(0)
            for tc in range(NTC):
                if tc + 1 < NTC:
                    emit_t2(tc + 1)
                emit_wo(tc)
            for tc in range(NTC):
                pp = tc % 2
                xn2v = xn2_sb[:, pp * D : (pp + 1) * D]
                if True:
                    un = f"xn2c{pp}_used"
                    P.res.setdefault(un, {})
                    extra = dict(P.res[un])
                    for r in (f"out_used{pp}", f"outv{pp}"):
                        for k, v in P.res.get(r, {}).items():
                            if extra.get(k, 0) < v:
                                extra[k] = v
                    P.res[un] = extra
                ln_stats_and_norm(tc, xn2v, f"xn2c{pp}", f"x1_{tc}")
                for dcg in range(2):
                    for d4 in range(4):
                        dc = dcg * 4 + d4
                        E("tensor",
                          (lambda dc_, d4_, xn2v_, pp_: lambda e: e.transpose(
                              PS[:, pp_ * 512 + d4_ * 128 : pp_ * 512 + (d4_ + 1) * 128],
                              xn2v_[:, dc_ * 128 : (dc_ + 1) * 128],
                              idf_sb[:, :]))(dc, d4, xn2v, pp),
                          ("pe", 1), after=(f"xn2c{pp}", f"b{pp}_free", "idf"), touch=(f"b{pp}",))
                    E("scalar",
                      (lambda dcg_, tc_, pp_: lambda e: e.copy(
                          xn2q_sb[:, :].rearrange("p (dc c) -> p dc c", dc=8)[
                              :, dcg_ * 4 : dcg_ * 4 + 4, tc_ * 128 : (tc_ + 1) * 128],
                          bank(pp_).rearrange("p (j c) -> p j c", j=4)))(dcg, tc, pp),
                      ("sc", 1), after=(f"b{pp}",),
                      touch=("x2q_part", f"b{pp}_free"))
                P.mark(f"xn2c{pp}_used", ("pe",))
            for dc in range(8):
                P.res[f"x2q_{dc}"] = dict(P.res["x2q_part"])
            w1t3 = w1t_sb[:, :].rearrange("p (dc h) -> p dc h", dc=8)
            xn2q3 = xn2q_sb[:, :].rearrange("p (dc t) -> p dc t", dc=8)
            for hc in range(32):
                pb = hc % 3
                for dcp in range(4):
                    E("tensor",
                      (lambda hc_, dcp_, pb_: lambda e: e.matmul(
                          bank(pb_),
                          w1t3[:, 2 * dcp_ : 2 * dcp_ + 2, hc_ * 128 : (hc_ + 1) * 128],
                          xn2q3[:, 2 * dcp_ : 2 * dcp_ + 2, :],
                          start=(dcp_ == 0), stop=(dcp_ == 3),
                          perf_mode=mybir.MatmulPerfMode.DoubleRow))(hc, dcp, pb),
                      ("pe", 1),
                      after=(f"x2q_{2 * dcp}", f"x2q_{2 * dcp + 1}", f"b{pb}_free", "w1t"),
                      touch=(f"b{pb}",))
                hs = hc % 2
                E("scalar",
                  (lambda hc_, pb_, hs_: lambda e: e.activation(
                      h16_sb[:, hs_ * TILE : (hs_ + 1) * TILE], bank(pb_),
                      AF.Relu, bias=b1q_sb[:, hc_ : hc_ + 1]))(hc, pb, hs),
                  ("sc", 1), after=(f"b{pb}", f"h16_used{hs}", "b1q"),
                  touch=(f"h16{hs}", f"b{pb}_free"))
                E("vector",
                  (lambda hc_, hs_: lambda e: e.tensor_copy(
                      h8_sb[:, hc_ * TILE : (hc_ + 1) * TILE],
                      h16_sb[:, hs_ * TILE : (hs_ + 1) * TILE]))(hc, hs),
                  ("dv", 1), after=(f"h16{hs}", "w2stage_used_1"),
                  touch=(f"h8_{hc}", f"h16_used{hs}"))
            w2t3 = w2t_sb[:, :].rearrange("p (hc o) -> p hc o", hc=32)
            h83 = h8_sb[:, :].rearrange("p (hc t) -> p hc t", hc=32)
            for oc in range(8):
                pb = oc % 3
                for hcp in range(16):
                    E("tensor",
                      (lambda oc_, hcp_, pb_: lambda e: e.matmul(
                          bank(pb_),
                          w2t3[:, 2 * hcp_ : 2 * hcp_ + 2, oc_ * 128 : (oc_ + 1) * 128],
                          h83[:, 2 * hcp_ : 2 * hcp_ + 2, :],
                          start=(hcp_ == 0), stop=(hcp_ == 15),
                          perf_mode=mybir.MatmulPerfMode.DoubleRow))(oc, hcp, pb),
                      ("pe", 1),
                      after=(f"h8_{2 * hcp}", f"h8_{2 * hcp + 1}", f"b{pb}_free", "w2t"),
                      touch=(f"b{pb}",))
                ys = oc % 2
                E("scalar",
                  (lambda oc_, pb_, ys_: lambda e: e.activation(
                      y16_sb[:, ys_ * TILE : (ys_ + 1) * TILE], bank(pb_),
                      AF.Identity, bias=b2q_sb[:, oc_ : oc_ + 1]))(oc, pb, ys),
                  ("sc", 1), after=(f"b{pb}", f"y16_used{ys}", "b2q"),
                  touch=(f"y16{ys}", f"b{pb}_free"))
                E("vector",
                  (lambda oc_, ys_: lambda e: e.tensor_tensor(
                      t_sb[:, oc_ * TILE : (oc_ + 1) * TILE],
                      y16_sb[:, ys_ * TILE : (ys_ + 1) * TILE],
                      mask_sb[:, oc_ * TILE : (oc_ + 1) * TILE], OP.mult))(oc, ys),
                  ("dv", 1), after=(f"y16{ys}", "mask", f"t_used_{oc}"),
                  touch=(f"t_{oc}", f"y16_used{ys}") + (("mask_used",) if oc == 7 else ()))
            if t + 1 < NT:
                emit_p2_mask(t + 1)
            for tc in range(NTC):
                pp = tc % 2
                outv_sl = out_sb[:, pp * D : (pp + 1) * D]
                pb7 = 1024 - pp * 1024
                bw = ("psb70_free", "psb70") if pp == 0 else ("psb6_free",)
                bf = "psb70_free" if pp == 0 else "psb6_free"
                for oc in range(8):
                    E("tensor",
                      (lambda oc_, tc_, pb7_: lambda e: e.transpose(
                          PSB[:, pb7_ + oc_ * 128 : pb7_ + (oc_ + 1) * 128],
                          t_sb[:, oc_ * TILE + tc_ * 128 : oc_ * TILE + (tc_ + 1) * 128],
                          idb_sb[:, :]))(oc, tc, pb7),
                      ("pe", 1), after=(f"t_{oc}", "idb") + bw, touch=("psb7o",))
                E("vector",
                  (lambda tc_, ov_, pb7_: lambda e: e.tensor_tensor(
                      ov_[:, :], PSB[:, pb7_ : pb7_ + 1024],
                      x_sb[:, tc_ * D : (tc_ + 1) * D], OP.add))(tc, outv_sl, pb7),
                  ("dv", 1),
                  after=("psb7o", f"x1_{tc}", f"out_used{pp}",
                         f"xn2c{pp}_used", f"xn2c{pp}"),
                  touch=(f"outv{pp}", bf) + (("p2x_used",) if tc == NTC - 1 else ()))
                E("gpsimd",
                  (lambda t_, tc_, ov_: lambda e: e.dma_start(
                      out=out_h[t_ * TILE + tc_ * 128 : t_ * TILE + (tc_ + 1) * 128, :],
                      in_=ov_[:, :]))(t, tc, outv_sl),
                  ("od", 16), after=(f"outv{pp}",), touch=(f"out_used{pp}",))
            for oc in range(8):
                P.mark(f"t_used_{oc}", ("pe",))

        if max_steps is not None:
            P.steps = P.steps[:max_steps]
        block = es.enter_context(nc.Block())
        final_gd = 0
        final_od = 0
        for engine, fn, waits, isem in P.steps:
            if isem == "gd":
                final_gd += 16
            elif isem == "od":
                final_od += 16

        def inc_amount(s):
            return 16 if s in ("ld", "gd", "lw", "gw", "od") else 1

        MAX_DMA_OUT = 16

        def run_engine(name):
            def body(eng):
                waited = {}
                issued = {}
                for engine, fn, waits, isem in P.steps:
                    if engine != name:
                        continue
                    for s, v in sorted(waits.items()):
                        if waited.get(s, 0) < v:
                            eng.wait_ge(sem[s], v)
                            waited[s] = v
                    if isem in ("ld", "gd", "lw", "gw", "od"):
                        n = issued.get(isem, 0)
                        if n - waited.get(isem, 0) // 16 >= MAX_DMA_OUT:
                            need = 16 * n
                            eng.wait_ge(sem[isem], need)
                            waited[isem] = need
                        issued[isem] = n + 1
                    fn(eng).then_inc(sem[isem], inc_amount(isem))
                if name == "gpsimd":
                    eng.wait_ge(sem["gd"], final_gd)
                    eng.wait_ge(sem["od"], final_od)
            return body

        block.sync(run_engine("sync"))
        block.vector(run_engine("vector"))
        block.scalar(run_engine("scalar"))
        block.tensor(run_engine("tensor"))
        block.gpsimd(run_engine("gpsimd"))

    return nc


_NC_CACHE = {}


def _get_nc(tokens_per_core):
    if tokens_per_core not in _NC_CACHE:
        _NC_CACHE[tokens_per_core] = build_nc(tokens_per_core)
    return _NC_CACHE[tokens_per_core]


def make_in_maps(inputs, tokens_per_core, n_seq_shards=4):
    X = np.ascontiguousarray(np.asarray(inputs["L0_attn_x"], np.float32))
    B, S, _ = X.shape
    Wq = np.ascontiguousarray(np.asarray(inputs["L0_attn_Wq"], np.float32))
    Wk = np.ascontiguousarray(np.asarray(inputs["L0_attn_Wk"], np.float32))
    Wv = np.ascontiguousarray(np.asarray(inputs["L0_attn_Wv"], np.float32))
    Wo = np.ascontiguousarray(np.asarray(inputs["L0_attn_Wo"], np.float32))
    proj = np.asarray(inputs["L0_attn_proj"], np.float32)
    w1 = np.ascontiguousarray(np.asarray(inputs["L0_ffn_w1"], np.float32))
    b1 = np.asarray(inputs["L0_ffn_b1"], np.float32)
    w2 = np.ascontiguousarray(np.asarray(inputs["L0_ffn_w2"], np.float32))
    b2 = np.asarray(inputs["L0_ffn_b2"], np.float32)

    proj2 = np.zeros((128, 128), np.float32)
    proj2[0:64, 0:64] = proj
    proj2[64:128, 64:128] = proj
    b1v = np.ascontiguousarray(b1.reshape(32, 128).T)
    b2v = np.ascontiguousarray(b2.reshape(8, 128).T)
    trilm = np.ascontiguousarray(
        np.tile(np.tril(np.ones((128, 128), np.float32)).T, (1, 4))
    ).astype(ml_dtypes.bfloat16)
    idf = np.eye(128, dtype=np.float32)
    idb = np.eye(128, dtype=np.float32).astype(ml_dtypes.bfloat16)
    i65 = np.eye(65, dtype=np.float32).astype(ml_dtypes.bfloat16)

    keep = _bernoulli_keep(42, 0.8, (B, S, D))
    mask8 = (keep.astype(np.float32) * np.float32(1.25)).astype(ml_dtypes.float8_e4m3)

    in_maps = []
    for c in range(N_CORES):
        b = c // n_seq_shards
        qt = c % n_seq_shards
        xs = np.ascontiguousarray(X[b, qt * tokens_per_core : (qt + 1) * tokens_per_core])
        ms = np.ascontiguousarray(
            mask8[b, qt * tokens_per_core : (qt + 1) * tokens_per_core].T
        )
        wpref = np.zeros((128, N_CORES), np.float32)
        for j in range(N_CORES):
            if j // n_seq_shards == b and j % n_seq_shards < qt:
                wpref[:, j] = 1.0
        in_maps.append(
            dict(x=xs, wq=Wq, wk=Wk, wv=Wv, wo=Wo, proj2=proj2, w1=w1, w2=w2,
                 b1v=b1v, b2v=b2v, dmask=ms, trilm=trilm, idf=idf, idb=idb,
                 i65=i65, wpref=wpref))
    return in_maps


def kernel(**inputs):
    X = np.asarray(inputs["L0_attn_x"])
    B, S, _ = X.shape
    tokens_per_core = (B * S) // N_CORES
    nc = _get_nc(tokens_per_core)
    in_maps = make_in_maps(inputs, tokens_per_core)
    res = run_bass_kernel_spmd(nc, in_maps, core_ids=list(range(N_CORES)))
    out = np.empty((B, S, D), np.float32)
    for c in range(N_CORES):
        b = c // 4
        qt = c % 4
        out[b, qt * tokens_per_core : (qt + 1) * tokens_per_core] = res.results[c]["out"]
    return out


# version 22 (final)
# speedup vs baseline: 1.0498x; 1.0498x over previous
import contextlib
import numpy as np
import ml_dtypes

import concourse.bass as bass
import concourse.mybir as mybir
from concourse.bass_utils import run_bass_kernel_spmd

F32 = mybir.dt.float32
BF16 = mybir.dt.bfloat16
F16 = mybir.dt.float16
F8 = mybir.dt.float8e4
AF = mybir.ActivationFunctionType
OP = mybir.AluOpType
AX = mybir.AxisListType

N_CORES = 8
D = 1024
H = 16
TILE = 512
CK = 128
NTC = TILE // CK
HFF = 4096


def _threefry2x32(k0, k1, x0, x1):
    R1 = (13, 15, 26, 6)
    R2 = (17, 29, 16, 24)
    ks0, ks1 = np.uint32(k0), np.uint32(k1)
    ks2 = np.uint32(ks0 ^ ks1 ^ np.uint32(0x1BD11BDA))
    x0 = x0 + ks0
    x1 = x1 + ks1
    ks = (ks1, ks2, ks0)

    def rotl(v, d):
        return (v << np.uint32(d)) | (v >> np.uint32(32 - d))

    for i in range(5):
        for r in R1 if i % 2 == 0 else R2:
            x0 = x0 + x1
            x1 = rotl(x1, r)
            x1 = x1 ^ x0
        x0 = x0 + ks[i % 3]
        x1 = x1 + ks[(i + 1) % 3] + np.uint32(i + 1)
    return x0, x1


def _bernoulli_keep(seed, p, shape):
    size = int(np.prod(shape))
    idx = np.arange(size, dtype=np.uint64)
    hi = (idx >> np.uint64(32)).astype(np.uint32)
    lo = (idx & np.uint64(0xFFFFFFFF)).astype(np.uint32)
    with np.errstate(over="ignore"):
        y0, y1 = _threefry2x32(0, seed, hi, lo)
    bits = y0 ^ y1
    f = ((bits >> np.uint32(9)) | np.uint32(0x3F800000)).view(np.float32) - np.float32(1.0)
    return (f < np.float32(p)).reshape(shape)


class Plan:
    def __init__(self):
        self.steps = []
        self.count = {"ld": 0, "gd": 0, "lw": 0, "gw": 0, "od": 0,
                      "pe": 0, "dv": 0, "sc": 0, "cc": 0}
        self.res = {}

    def emit(self, engine, fn, inc, after=(), touch=()):
        waits = {}
        for r in after:
            for s, v in self.res.get(r, {}).items():
                if waits.get(s, 0) < v:
                    waits[s] = v
        for s in ("ld", "gd", "lw", "gw", "od"):
            if s in waits:
                waits[s] = self.count[s]
        sem, n = inc
        self.count[sem] += n
        self.steps.append((engine, fn, waits, sem))
        done = {sem: self.count[sem]}
        for r in touch:
            self.res[r] = dict(done)

    def mark(self, name, sems):
        self.res[name] = {s: self.count[s] for s in sems}


def build_nc(tokens_per_core, max_steps=None):
    Tc = tokens_per_core
    NT = Tc // TILE
    nc = bass.Bass("TRN2", target_bir_lowering=False)

    x_h = nc.dram_tensor("x", [Tc, D], F32, kind="ExternalInput").ap()
    wq_h = nc.dram_tensor("wq", [D, D], F32, kind="ExternalInput").ap()
    wk_h = nc.dram_tensor("wk", [D, D], F32, kind="ExternalInput").ap()
    wv_h = nc.dram_tensor("wv", [D, D], F32, kind="ExternalInput").ap()
    wo_h = nc.dram_tensor("wo", [D, D], F32, kind="ExternalInput").ap()
    proj2_h = nc.dram_tensor("proj2", [128, 128], F32, kind="ExternalInput").ap()
    w1_h = nc.dram_tensor("w1", [HFF, D], F32, kind="ExternalInput").ap()
    w2_h = nc.dram_tensor("w2", [D, HFF], F32, kind="ExternalInput").ap()
    b1_h = nc.dram_tensor("b1v", [128, 32], F32, kind="ExternalInput").ap()
    b2_h = nc.dram_tensor("b2v", [128, 8], F32, kind="ExternalInput").ap()
    mask_h = nc.dram_tensor("dmask", [D, Tc], F8, kind="ExternalInput").ap()
    tril_h = nc.dram_tensor("trilm", [128, 512], BF16, kind="ExternalInput").ap()
    idf_h = nc.dram_tensor("idf", [128, 128], F32, kind="ExternalInput").ap()
    idb_h = nc.dram_tensor("idb", [128, 128], BF16, kind="ExternalInput").ap()
    i65_h = nc.dram_tensor("i65", [65, 65], BF16, kind="ExternalInput").ap()
    wpref_h = nc.dram_tensor("wpref", [128, N_CORES], F32, kind="ExternalInput").ap()
    out_h = nc.dram_tensor("out", [Tc, D], F32, kind="ExternalOutput").ap()

    zloc_h = nc.dram_tensor("zloc_st", [65, NT * H * TILE], BF16)
    qfst_h = nc.dram_tensor("qf_st", [64, NT * H * TILE], BF16)
    kvloc_h = nc.dram_tensor("kvloc", [64, H * 65], F32)
    kvag_h = nc.dram_tensor("kvag", [N_CORES * 64, H * 65], F32)

    es = contextlib.ExitStack()
    P = Plan()
    with es:
        sbT = lambda name, shape, dt: es.enter_context(nc.sbuf_tensor(name, shape, dt))
        x_sb = sbT("x_sb", [128, 4 * D], F32)
        wo_sb = sbT("wo_sb", [128, 8 * D], BF16)
        proj2_sb = sbT("proj2_sb", [128, 128], BF16)
        idf_sb = sbT("idf_sb", [128, 128], F32)
        idb_sb = sbT("idb_sb", [128, 128], BF16)
        i65_sb = sbT("i65_sb", [65, 65], BF16)
        tril_sb = sbT("tril_sb", [128, 512], BF16)
        wpref_sb = sbT("wpref_sb", [128, N_CORES], F32)
        b1q_sb = sbT("b1q_sb", [128, 32], F32)
        b2q_sb = sbT("b2q_sb", [128, 8], F32)
        bq8_sb = sbT("bq8_sb", [128, 32], F8)
        stat_sb = sbT("stat_sb", [128, 40], F32)
        KV_sb = sbT("KV_sb", [64, H * 65], F32)
        KVc_sb = sbT("KVc_sb", [64, H * 65], BF16)
        KVp_sb = sbT("KVp_sb", [64, H * 65], BF16)
        zloc_sb = sbT("zloc_sb", [65, H * TILE], BF16)
        qf_sb = sbT("qf_sb", [64, H * TILE], BF16)
        kvag_buf = qf_sb[0:64, 0 : 2 * H * 65].bitcast(F32)
        kvpA_sb = qf_sb[0:64, 2 * H * 65 : 4 * H * 65].bitcast(F32)
        kvpB_sb = qf_sb[0:64, 4 * H * 65 : 6 * H * 65].bitcast(F32)

        PS = es.enter_context(nc.psum_tensor("PSf", [128, 3072], F32))
        PSB = es.enter_context(nc.psum_tensor("PSb", [128, 2048], BF16))
        bank = lambda i: PS[:, 512 * i : 512 * (i + 1)]

        ph1 = contextlib.ExitStack()
        s1 = lambda name, shape, dt: ph1.enter_context(nc.sbuf_tensor(name, shape, dt))
        wq_sb = s1("wq_sb", [128, 8 * D], BF16)
        wk_sb = s1("wk_sb", [128, 8 * D], BF16)
        wv_sb = s1("wv_sb", [128, 8 * D], BF16)
        xn_sb = s1("xn_sb", [128, 8 * D], BF16)
        xnf_sb = s1("xnf_sb", [128, 8 * TILE], BF16)
        q1_sb = s1("q1_sb", [128, 2 * TILE], BF16)
        k1_sb = s1("k1_sb", [128, 2 * TILE], BF16)
        kf_sb = s1("kf_sb", [64, H * TILE], BF16)
        ktm_sb = s1("ktm_sb", [128, NTC * 1024], BF16)
        vaug_sb = s1("vaug_sb", [128, NTC * H * 65], BF16)
        am_sb = s1("am_sb", [128, 2 * 512], BF16)
        elu_e = s1("elu_e", [64, 2 * 1024], BF16)
        elu_g = s1("elu_g", [64, 2 * 1024], F32)
        ktm_e = s1("ktm_e", [128, 2 * 512], BF16)
        ktm_g = s1("ktm_g", [128, 2 * 512], F32)
        ph1.close()

        w1t_sb = sbT("w1t_sb", [128, 8 * HFF], F8)
        w2t_sb = sbT("w2t_sb", [128, 32 * D], F8)
        xn2_sb = sbT("xn2_sb", [128, 2 * D], F32)
        xn2q_sb = sbT("xn2q_sb", [128, 8 * TILE], F8)
        h8_sb = sbT("h8_sb", [128, 32 * TILE], F8)
        h16_sb = sbT("h16_sb", [128, 2 * TILE], F16)
        y16_sb = sbT("y16_sb", [128, 2 * TILE], F16)
        t_sb = sbT("t_sb", [128, 8 * TILE], BF16)
        mask_sb = sbT("mask_sb", [128, 8 * TILE], F8)
        zim_sb = sbT("zim_sb", [65, 1024], BF16)
        ztm_sb = sbT("ztm_sb", [128, 4096], BF16)
        zfm_sb = sbT("zfm_sb", [128, 4096], BF16)
        rden_sb = sbT("rden_sb", [128, 16], F32)
        out_sb = xn2_sb

        sem = {}
        for s in ("ld", "gd", "lw", "gw", "od", "pe", "dv", "sc", "cc"):
            sem[s] = es.enter_context(nc.semaphore(s))

        E = P.emit

        wstA = xn_sb[:, 4 * D : 6 * D].bitcast(F32)
        wstB = xn_sb[:, 6 * D : 8 * D].bitcast(F32)

        for dst, src, r in (
            (tril_sb[:, :], tril_h, "tril"),
            (idf_sb[:, :], idf_h, "idf"),
            (idb_sb[:, :], idb_h, "idb"),
            (i65_sb[:, :], i65_h, "i65"),
            (wpref_sb[:, :], wpref_h, "wpref"),
            (b1q_sb[:, :], b1_h, "b1raw"),
            (b2q_sb[:, :], b2_h, "b2raw"),
        ):
            E("sync", (lambda d, s: lambda e: e.dma_start(out=d, in_=s))(dst, src),
              ("ld", 16), touch=(r,))

        E("sync", lambda e: e.dma_start(out=wstA[:, 0:128], in_=proj2_h),
          ("lw", 16), touch=("pstage",))
        E("vector", lambda e: e.tensor_copy(proj2_sb[:, :], wstA[:, 0:128]),
          ("dv", 1), after=("pstage",), touch=("proj2", "wst0_c"))
        E("vector", lambda e: e.memset(stat_sb[:, 6:7], 1e-5), ("dv", 1), touch=("epsc",))
        E("vector", lambda e: e.memset(vaug_sb[:, 64 : NTC * H * 65 : 65], 1.0),
          ("dv", 1), touch=("vaug_ones",))

        def ln_stats_and_norm(tc, out_tile, out_res, src_res, extra=()):
            pp = tc % 2
            xs = x_sb[:, tc * D : (tc + 1) * D]
            c6 = stat_sb[:, 16 + 12 * pp : 28 + 12 * pp]
            b0 = pp * 8
            c2 = stat_sb[:, b0 : b0 + 2]
            c_nm = stat_sb[:, b0 + 2 : b0 + 3]
            c_t1 = stat_sb[:, b0 + 4 : b0 + 5]
            c_r = stat_sb[:, b0 + 5 : b0 + 6]
            E("vector",
              (lambda xs_, c6_: lambda e: e.bn_stats(c6_[:, 0:6], xs_[:, 0:512]))(xs, c6),
              ("dv", 1), after=(src_res, f"stats_used{pp}"), touch=(f"st_6{pp}",))
            E("vector",
              (lambda xs_, c6_: lambda e: e.bn_stats(c6_[:, 6:12], xs_[:, 512:1024]))(xs, c6),
              ("dv", 1), after=(src_res, f"st_6{pp}"), touch=(f"st_6b{pp}",))
            E("vector", (lambda c2_, c6_: lambda e: e.bn_aggr(c2_, c6_))(c2, c6),
              ("dv", 1), after=(f"st_6{pp}", f"st_6b{pp}"), touch=(f"st_2{pp}",))
            E("vector",
              (lambda c_nm_, c2_: lambda e: e.tensor_scalar_mul(c_nm_, c2_[:, 0:1], -1.0))(c_nm, c2),
              ("dv", 1), after=(f"st_2{pp}",), touch=(f"st_nm{pp}",))
            E("scalar",
              (lambda c_t1_, c2_: lambda e: e.activation(
                  c_t1_, c2_[:, 1:2], AF.Ln, bias=stat_sb[:, 6:7]))(c_t1, c2),
              ("sc", 1), after=(f"st_2{pp}", "epsc"), touch=(f"st_ln{pp}",))
            E("scalar",
              (lambda c_r_, c_t1_: lambda e: e.activation(c_r_, c_t1_, AF.Exp, scale=-0.5))(c_r, c_t1),
              ("sc", 1), after=(f"st_ln{pp}",), touch=(f"st_r{pp}",))
            E("vector",
              (lambda xs_, ot, c_nm_, c_r_: lambda e: e.tensor_scalar(
                  ot, xs_, c_nm_, c_r_, OP.add, OP.mult))(xs, out_tile, c_nm, c_r),
              ("dv", 1), after=(f"st_r{pp}", f"st_nm{pp}", out_res + "_used") + extra,
              touch=(out_res, f"stats_used{pp}"))

        def emit_xln(t):
            tok0 = t * TILE
            xnsl = (t % 2) * 4 * D
            for tc in range(NTC):
                E("sync",
                  (lambda tc_, tok0_: lambda e: e.dma_start(
                      out=x_sb[:, tc_ * D : (tc_ + 1) * D],
                      in_=x_h[tok0_ + tc_ * 128 : tok0_ + (tc_ + 1) * 128, :]))(tc, tok0),
                  ("ld", 16), after=("x_used",), touch=(f"x_{tc}",))
            for tc in range(NTC):
                ln_stats_and_norm(tc, xn_sb[:, xnsl + tc * D : xnsl + (tc + 1) * D],
                                  f"xn{t % 2}_{tc}", f"x_{tc}",
                                  extra=("w3",) if t == 1 else ())
            P.mark("x_used", ("dv", "sc"))

        emit_xln(0)

        E("vector", lambda e: e.tensor_copy(bq8_sb[:, 0:32], b1q_sb[:, :]),
          ("dv", 1), after=("b1raw",), touch=("b8",))
        E("vector", lambda e: e.tensor_copy(b1q_sb[:, :], bq8_sb[:, 0:32]),
          ("dv", 1), after=("b8",), touch=("b1q", "b8f"))
        E("vector", lambda e: e.tensor_copy(bq8_sb[:, 0:8], b2q_sb[:, :]),
          ("dv", 1), after=("b2raw", "b8f"), touch=("b82",))
        E("vector", lambda e: e.tensor_copy(b2q_sb[:, :], bq8_sb[:, 0:8]),
          ("dv", 1), after=("b82",), touch=("b2q",))

        whs = (wq_h, wk_h, wv_h, wo_h)
        wsbs = (wq_sb, wk_sb, wv_sb, wo_sb)

        def emit_wpair(wi, dc):
            p = dc % 2
            stg = wstA if p == 0 else wstB
            E("sync",
              (lambda wh_, dc_, stg_: lambda e: e.dma_start(
                  out=stg_[:, :], in_=wh_[dc_ * 128 : (dc_ + 1) * 128, :]))(whs[wi], dc, stg),
              ("lw" if p == 0 else "gw", 16), after=(f"wst{p}_c",), touch=(f"wst{p}",))
            fin = ("w3",) if wi == 3 and dc == 7 else ()
            if wi in (0, 1):
                E("vector",
                  (lambda wsb_, dc_, stg_: lambda e: e.tensor_copy(
                      wsb_[:, dc_ * D : (dc_ + 1) * D], stg_[:, :]))(wsbs[wi], dc, stg),
                  ("dv", 1), after=(f"wst{p}",), touch=(f"wst{p}_c", f"w{wi}_{dc}") + fin)
            else:
                E("scalar",
                  (lambda wsb_, dc_, stg_: lambda e: e.copy(
                      wsb_[:, dc_ * D : (dc_ + 1) * D], stg_[:, :]))(wsbs[wi], dc, stg),
                  ("sc", 1), after=(f"wst{p}",), touch=(f"wst{p}_c", f"w{wi}_{dc}") + fin)

        for dc in range(8):
            emit_wpair(0, dc)
        for dc in range(8):
            emit_wpair(1, dc)
        wpair_at_oc = {2: [(2, 0), (2, 1)], 3: [(2, 2), (2, 3)],
                       4: [(2, 4), (2, 5)], 5: [(2, 6), (2, 7)],
                       6: [(3, 0), (3, 1), (3, 2), (3, 3)],
                       7: [(3, 4), (3, 5), (3, 6), (3, 7)]}
        for t in range(NT):
            xnsl = (t % 2) * 4 * D
            for dc in range(8):
                for tc in range(NTC):
                    E("tensor",
                      (lambda dc_, tc_, xnsl_: lambda e: e.transpose(
                          PSB[:, tc_ * 128 : (tc_ + 1) * 128],
                          xn_sb[:, xnsl_ + tc_ * D + dc_ * 128 : xnsl_ + tc_ * D + (dc_ + 1) * 128],
                          idb_sb[:, :]))(dc, tc, xnsl),
                      ("pe", 1), after=(f"xn{t % 2}_{tc}", "psb6_free", "idb"), touch=("psb6",))
                E("scalar",
                  (lambda dc_: lambda e: e.copy(xnf_sb[:, dc_ * TILE : (dc_ + 1) * TILE], PSB[:, 0:512]))(dc),
                  ("sc", 1), after=("psb6",), touch=(f"xnf_{dc}", "psb6_free"))
            for tc in range(NTC):
                P.mark(f"xn{t % 2}_{tc}_used", ("pe",))
            def emit_proj(oc):
                sl = oc % 2
                q1v = q1_sb[:, sl * TILE : (sl + 1) * TILE]
                k1v = k1_sb[:, sl * TILE : (sl + 1) * TILE]
                bq = 5 if oc % 2 == 0 else 3
                bk = 3 if oc % 2 == 0 else 5
                for dc in range(8):
                    E("tensor",
                      (lambda oc_, dc_, bq_: lambda e: e.matmul(
                          bank(bq_),
                          wq_sb[:, dc_ * D + oc_ * 128 : dc_ * D + (oc_ + 1) * 128],
                          xnf_sb[:, dc_ * TILE : (dc_ + 1) * TILE],
                          start=(dc_ == 0), stop=(dc_ == 7)))(oc, dc, bq),
                      ("pe", 1), after=(f"xnf_{dc}", f"b{bq}_free", f"w0_{dc}"), touch=(f"b{bq}",))
                E("scalar", (lambda bq_, q1v_: lambda e: e.copy(q1v_, bank(bq_)))(bq, q1v),
                  ("sc", 1), after=(f"b{bq}", f"q1_used{sl}"), touch=(f"q1_{sl}", f"b{bq}_free"))
                for dc in range(8):
                    E("tensor",
                      (lambda oc_, dc_, bk_: lambda e: e.matmul(
                          bank(bk_),
                          wk_sb[:, dc_ * D + oc_ * 128 : dc_ * D + (oc_ + 1) * 128],
                          xnf_sb[:, dc_ * TILE : (dc_ + 1) * TILE],
                          start=(dc_ == 0), stop=(dc_ == 7)))(oc, dc, bk),
                      ("pe", 1), after=(f"xnf_{dc}", f"b{bk}_free", f"w1_{dc}"), touch=(f"b{bk}",))
                E("scalar", (lambda bk_, k1v_: lambda e: e.copy(k1v_, bank(bk_)))(bk, k1v),
                  ("sc", 1), after=(f"b{bk}", f"k1_used{sl}"), touch=(f"k1_{sl}", f"b{bk}_free"))

            def emit_fmktm(oc):
                sl = oc % 2
                q1v = q1_sb[:, sl * TILE : (sl + 1) * TILE]
                k1v = k1_sb[:, sl * TILE : (sl + 1) * TILE]
                for which, src_res, src_sb, dst_sb, dst_res, slot in (
                    ("q", f"q1_{sl}", q1v, qf_sb, "qf", 0),
                    ("k", f"k1_{sl}", k1v, kf_sb, "kf", 1),
                ):
                    fb = 0 if (oc + slot) % 2 == 0 else 2
                    pf = PS[0:64, fb * 512 : fb * 512 + 1024]
                    ee = elu_e[:, slot * 1024 : (slot + 1) * 1024]
                    gg = elu_g[:, slot * 1024 : (slot + 1) * 1024]
                    for half in range(2):
                        E("tensor",
                          (lambda half_, fb_, src_: lambda e: e.matmul(
                              PS[0:64, (fb_ + half_) * 512 : (fb_ + half_) * 512 + 512],
                              proj2_sb[:, half_ * 64 : half_ * 64 + 64],
                              src_[:, :], start=True, stop=True))(half, fb, src_sb),
                          ("pe", 1),
                          after=(src_res, f"b{fb + half}_free", "proj2"),
                          touch=(f"b{fb + half}",) + ((f"q1_used{sl}",) if which == "q" and half == 1 else ()))
                    E("scalar",
                      (lambda pf_, ee_: lambda e: e.activation(ee_, pf_, AF.Exp))(pf, ee),
                      ("sc", 1), after=(f"b{fb}", f"b{fb + 1}", f"elu_e_used{slot}"),
                      touch=(f"elu_e{slot}",))
                    E("scalar",
                      (lambda pf_, gg_: lambda e: e.activation(gg_, pf_, AF.Relu))(pf, gg),
                      ("sc", 1), after=(f"b{fb}", f"b{fb + 1}", f"elu_g_used{slot}"),
                      touch=(f"elu_g{slot}", f"b{fb}_free", f"b{fb + 1}_free"))
                    E("vector",
                      (lambda oc_, ee_, gg_, dst_: lambda e: e.scalar_tensor_tensor(
                          dst_[0:64, oc_ * 1024 : (oc_ + 1) * 1024], ee_, 1.0, gg_,
                          OP.min, OP.add))(oc, ee, gg, dst_sb),
                      ("dv", 1), after=(f"elu_e{slot}", f"elu_g{slot}")
                      + ((f"qf_used2_h{oc // 4}",) if which == "q" else ()),
                      touch=(dst_res, f"elu_e_used{slot}", f"elu_g_used{slot}")
                      + ((f"qfh{oc // 4}",) if which == "q" and oc % 4 == 3 else ()))
                for tc in range(NTC):
                    E("tensor",
                      (lambda tc_, k1v_: lambda e: e.matmul(
                          PS[:, 2048 + tc_ * 128 : 2048 + (tc_ + 1) * 128],
                          k1v_[:, tc_ * 128 : (tc_ + 1) * 128],
                          proj2_sb[:, :], start=True, stop=True))(tc, k1v),
                      ("pe", 1), after=(f"k1_{sl}", "b4_free", "proj2"),
                      touch=("b4",) + ((f"k1_used{sl}",) if tc == NTC - 1 else ()))
                ks_ = oc % 2
                kte = ktm_e[:, ks_ * 512 : (ks_ + 1) * 512]
                ktg = ktm_g[:, ks_ * 512 : (ks_ + 1) * 512]
                E("scalar", (lambda kte_: lambda e: e.activation(kte_, bank(4), AF.Exp))(kte),
                  ("sc", 1), after=("b4", f"ktm_e_used{ks_}"), touch=(f"ktm_e{ks_}",))
                E("scalar", (lambda ktg_: lambda e: e.activation(ktg_, bank(4), AF.Relu))(ktg),
                  ("sc", 1), after=("b4", f"ktm_g_used{ks_}"),
                  touch=(f"ktm_g{ks_}", "b4_free"))
                E("vector",
                  (lambda oc_, kte_, ktg_: lambda e: e.scalar_tensor_tensor(
                      ktm_sb[:, :].rearrange("p (t c) -> p t c", t=NTC)[:, :, oc_ * 128 : (oc_ + 1) * 128],
                      kte_.rearrange("p (t c) -> p t c", t=NTC), 1.0,
                      ktg_.rearrange("p (t c) -> p t c", t=NTC),
                      OP.min, OP.add))(oc, kte, ktg),
                  ("dv", 1), after=(f"ktm_e{ks_}", f"ktm_g{ks_}"),
                  touch=("ktm", f"ktm_e_used{ks_}", f"ktm_g_used{ks_}"))
                if oc in (3, 7):
                    hh = oc // 4
                    E("gpsimd",
                      (lambda t_, hh_: lambda e: e.dma_start(
                          out=qfst_h[:, t_ * H * TILE + hh_ * 4096
                                     : t_ * H * TILE + (hh_ + 1) * 4096],
                          in_=qf_sb[0:64, hh_ * 4096 : (hh_ + 1) * 4096]))(t, hh),
                      ("gd", 16), after=(f"qfh{hh}",),
                      touch=(f"qf_used2_h{hh}",)
                      + (("qf_stashed", "qf_used2") if hh == 1 else ()))

            def emit_kvc(tc):
                if t == 0 and tc == 0:
                    E("vector", lambda e: e.memset(KV_sb[:, :], 0.0),
                      ("dv", 1), after=("kv_used",), touch=("kv_state",))
                E("vector", lambda e: e.tensor_copy(KVc_sb[:, :], KV_sb[:, :]),
                  ("dv", 1), after=("kv_state", "kvc_used"), touch=("kvc",))

            def emit_A(g, tc):
                ab = 4 if g % 2 == 0 else 2
                asl = g % 2
                amv = am_sb[:, asl * 512 : (asl + 1) * 512]
                for hl in range(4):
                    h = g * 4 + hl
                    E("tensor",
                      (lambda h_, hl_, tc_, ab_: lambda e: e.matmul(
                          PS[:, ab_ * 512 + hl_ * 128 : ab_ * 512 + (hl_ + 1) * 128],
                          kf_sb[0:64, h_ * TILE + tc_ * 128 : h_ * TILE + (tc_ + 1) * 128],
                          qf_sb[0:64, h_ * TILE + tc_ * 128 : h_ * TILE + (tc_ + 1) * 128],
                          start=True, stop=True))(h, hl, tc, ab),
                      ("pe", 1), after=("kf", "qf", f"b{ab}_free"), touch=(f"b{ab}",))
                E("vector",
                  (lambda ab_, amv_: lambda e: e.tensor_tensor(amv_, bank(ab_), tril_sb[:, :], OP.mult))(ab, amv),
                  ("dv", 1), after=(f"b{ab}", "tril", f"am_used{asl}"),
                  touch=(f"am{asl}", f"b{ab}_free"))

            def emit_z(g, tc):
                zb = 5 if g % 2 == 0 else 3
                asl = g % 2
                amv = am_sb[:, asl * 512 : (asl + 1) * 512]
                for hl in range(4):
                    h = g * 4 + hl
                    E("tensor",
                      (lambda h_, hl_, tc_, zb_, amv_: lambda e: e.matmul(
                          PS[0:65, zb_ * 512 + hl_ * 128 : zb_ * 512 + (hl_ + 1) * 128],
                          vaug_sb[:, tc_ * H * 65 + h_ * 65 : tc_ * H * 65 + (h_ + 1) * 65],
                          amv_[:, hl_ * 128 : (hl_ + 1) * 128],
                          start=(hl_ == 0), stop=False))(h, hl, tc, zb, amv),
                      ("pe", 1),
                      after=(f"am{asl}", f"vaug_{tc}_{g // 2}", f"b{zb}_free", "kvc", "qf"),
                      touch=(f"b{zb}z",))
                    E("tensor",
                      (lambda h_, hl_, tc_, zb_: lambda e: e.matmul(
                          PS[0:65, zb_ * 512 + hl_ * 128 : zb_ * 512 + (hl_ + 1) * 128],
                          KVc_sb[:, h_ * 65 : (h_ + 1) * 65],
                          qf_sb[0:64, h_ * TILE + tc_ * 128 : h_ * TILE + (tc_ + 1) * 128],
                          start=False, stop=(hl_ == 3)))(h, hl, tc, zb),
                      ("pe", 1), touch=(f"b{zb}z",))
                E("scalar",
                  (lambda tc_, g_, zb_: lambda e: e.copy(
                      zloc_sb[0:65, (tc_ * 4 + g_) * 512 : (tc_ * 4 + g_ + 1) * 512],
                      PS[0:65, zb_ * 512 : zb_ * 512 + 512]))(tc, g, zb),
                  ("sc", 1), after=(f"b{zb}z", f"zloc_used_tc{tc}"),
                  touch=("zloc", f"b{zb}_free", f"am_used{asl}")
                  + ((f"zloc_tc{tc}",) if g == 3 else ()))

            def emit_kv(g4, tc):
                for hh in range(4):
                    h = g4 * 4 + hh
                    E("tensor",
                      (lambda h_, tc_, g4_: lambda e: e.matmul(
                          PS[0:64, g4_ * 512 + (h_ % 4) * 65 : g4_ * 512 + (h_ % 4) * 65 + 65],
                          ktm_sb[:, tc_ * 1024 + h_ * 64 : tc_ * 1024 + (h_ + 1) * 64],
                          vaug_sb[:, tc_ * H * 65 + h_ * 65 : tc_ * H * 65 + (h_ + 1) * 65],
                          start=(h_ % 4 == 0),
                          stop=(h_ % 4 == 3)))(h, tc, g4),
                      ("pe", 1),
                      after=("ktm", f"vaug_{tc}_{g4 // 2}", "kvc", f"b{g4}_free", "fold_done"),
                      touch=(f"pk_{g4}", f"b{g4}"))

            for oc in range(8):
                emit_proj(oc)
                if oc >= 1:
                    emit_fmktm(oc - 1)
                if t == 0:
                    for wi, dc in wpair_at_oc.get(oc, ()):
                        emit_wpair(wi, dc)
            emit_fmktm(7)
            emit_kvc(0)
            emit_A(0, 0)
            for tc in range(NTC):
                for half in range(2):
                    vb = 5 if (tc * 2 + half) % 2 == 0 else 3
                    for dc in range(8):
                        E("tensor",
                          (lambda tc_, half_, dc_, vb_: lambda e: e.matmul(
                              bank(vb_),
                              xnf_sb[:, dc_ * TILE + tc_ * 128 : dc_ * TILE + (tc_ + 1) * 128],
                              wv_sb[:, dc_ * D + half_ * 512 : dc_ * D + half_ * 512 + 512],
                              start=(dc_ == 0), stop=(dc_ == 7)))(tc, half, dc, vb),
                          ("pe", 1), after=(f"xnf_{dc}", f"b{vb}_free", f"w2_{dc}"), touch=(f"b{vb}",))
                    E("vector",
                      (lambda tc_, half_, vb_: lambda e: e.tensor_copy(
                          vaug_sb[:, :].rearrange("p (t h c) -> p t h c", t=NTC, h=H)[
                              :, tc_, half_ * 8 : half_ * 8 + 8, 0:64],
                          bank(vb_).rearrange("p (h c) -> p h c", h=8)))(tc, half, vb),
                      ("dv", 1), after=(f"b{vb}", "vaug_ones", "vaug_used"),
                      touch=("vaug", f"vaug_{tc}_{half}", f"b{vb}_free"))
            if t + 1 < NT:
                emit_xln(t + 1)
            for tc in range(NTC):
                if tc > 0:
                    emit_kvc(tc)
                if tc > 0:
                    emit_A(0, tc)
                for g in range(4):
                    if g + 1 < 4:
                        emit_A(g + 1, tc)
                    emit_z(g, tc)
                    emit_kv(g, tc)
                for g4 in range(4):
                    E("vector",
                      (lambda g4_: lambda e: e.tensor_tensor(
                          KV_sb[:, g4_ * 260 : (g4_ + 1) * 260],
                          PS[0:64, g4_ * 512 : g4_ * 512 + 260],
                          KV_sb[:, g4_ * 260 : (g4_ + 1) * 260], OP.add))(g4),
                      ("dv", 1), after=(f"pk_{g4}", "kv_state"),
                      touch=("kv_state", "kvc_used", "fold_done", f"b{g4}_free"))
                E("gpsimd",
                  (lambda t_, tc_: lambda e: e.dma_start(
                      out=zloc_h[:, t_ * H * TILE + tc_ * 2048
                                 : t_ * H * TILE + (tc_ + 1) * 2048],
                      in_=zloc_sb[:, tc_ * 2048 : (tc_ + 1) * 2048]))(t, tc),
                  ("gd", 16), after=(f"zloc_tc{tc}",),
                  touch=(f"zloc_used_tc{tc}", "zloc_stashed"))
            P.mark("vaug_used", ("pe",))

        P.mark("ph1_pe_done", ("pe",))
        h8f32 = h8_sb[:, :].bitcast(F32)
        stgs = (x_sb, h8f32)
        for hg in range(8):
            p = hg % 2
            stg = stgs[p]
            for j in range(4):
                E("sync",
                  (lambda hg_, j_, stg_: lambda e: e.dma_start(
                      out=stg_[:, j_ * D : (j_ + 1) * D],
                      in_=w1_h[hg_ * 512 + j_ * 128 : hg_ * 512 + (j_ + 1) * 128, :]))(hg, j, stg),
                  ("gw" if p else "lw", 16),
                  after=(f"w1stage_used_{p}", "x_used") + (("ph1_pe_done",) if p else ()),
                  touch=(f"w1st_{p}_{j}",))
            for dc in range(8):
                pbk = 4 if dc % 2 == 0 else 2
                for j in range(4):
                    E("tensor",
                      (lambda dc_, j_, pbk_, stg_: lambda e: e.transpose(
                          PS[:, pbk_ * 512 + j_ * 128 : pbk_ * 512 + (j_ + 1) * 128],
                          stg_[:, j_ * D + dc_ * 128 : j_ * D + (dc_ + 1) * 128],
                          idf_sb[:, :]))(dc, j, pbk, stg),
                      ("pe", 1), after=(f"w1st_{p}_{j}", f"b{pbk}_free", "idf"), touch=(f"b{pbk}",))
                E("scalar",
                  (lambda hg_, dc_, pbk_: lambda e: e.copy(
                      w1t_sb[:, dc_ * HFF + hg_ * 512 : dc_ * HFF + (hg_ + 1) * 512],
                      bank(pbk_)))(hg, dc, pbk),
                  ("sc", 1), after=(f"b{pbk}",),
                  touch=("w1t", f"b{pbk}_free") + ((f"w1stage_used_{p}",) if dc == 7 else ()))
        for og in range(8):
            p = og % 2
            stg = stgs[p]
            E("sync",
              (lambda og_, stg_: lambda e: e.dma_start(
                  out=stg_[:, :], in_=w2_h[og_ * 128 : (og_ + 1) * 128, :]))(og, stg),
              ("gw" if p else "lw", 16),
              after=(f"w2stage_used_{p}", f"w1stage_used_{p}"), touch=(f"w2st_{p}",))
            for hcg in range(8):
                pbk = 4 if hcg % 2 == 0 else 2
                for j in range(4):
                    E("tensor",
                      (lambda hcg_, j_, pbk_, stg_: lambda e: e.transpose(
                          PS[:, pbk_ * 512 + j_ * 128 : pbk_ * 512 + (j_ + 1) * 128],
                          stg_[:, (hcg_ * 4 + j_) * 128 : (hcg_ * 4 + j_ + 1) * 128],
                          idf_sb[:, :]))(hcg, j, pbk, stg),
                      ("pe", 1), after=(f"w2st_{p}", f"b{pbk}_free", "idf"), touch=(f"b{pbk}",))
                E("scalar",
                  (lambda og_, hcg_, pbk_: lambda e: e.copy(
                      w2t_sb[:, :].rearrange("p (hc c) -> p hc c", hc=32)[
                          :, hcg_ * 4 : hcg_ * 4 + 4, og_ * 128 : (og_ + 1) * 128],
                      bank(pbk_).rearrange("p (j c) -> p j c", j=4)))(og, hcg, pbk),
                  ("sc", 1), after=(f"b{pbk}",),
                  touch=("w2t", f"b{pbk}_free") + ((f"w2stage_used_{p}",) if hcg == 7 else ()))

        E("gpsimd", lambda e: e.dma_start(out=kvloc_h[:, :], in_=KV_sb[:, :]),
          ("gd", 16), after=("kv_state",), touch=("kvloc_d", "kv_used"))
        E("gpsimd",
          lambda e: e.collective_compute(
              "AllGather", OP.bypass, replica_groups=[list(range(N_CORES))],
              ins=[kvloc_h.ap().opt()], outs=[kvag_h.ap().opt()]),
          ("cc", 1), after=("kvloc_d",), touch=("kvag_d",))
        E("vector", lambda e: e.memset(kvpA_sb[:, :], 0.0), ("dv", 1), after=("qf_stashed",), touch=("kvp0",))
        for j in range(N_CORES):
            src, dst = (kvpA_sb, kvpB_sb) if j % 2 == 0 else (kvpB_sb, kvpA_sb)
            E("gpsimd",
              (lambda j_: lambda e: e.dma_start(
                  out=kvag_buf[:, :], in_=kvag_h.ap()[j_ * 64 : (j_ + 1) * 64, :]))(j),
              ("gd", 16), after=("kvag_d", f"kvagb_used", "qf_stashed"), touch=("kvagb",))
            E("vector",
              (lambda j_, s_, d_: lambda e: e.scalar_tensor_tensor(
                  d_[:, :], kvag_buf[:, :],
                  wpref_sb[0:64, j_ : j_ + 1], s_[:, :], OP.mult, OP.add))(j, src, dst),
              ("dv", 1), after=("kvagb", "wpref", f"kvp{j}"), touch=(f"kvp{j + 1}", "kvagb_used"))
        kvp_fin = kvpA_sb if N_CORES % 2 == 0 else kvpB_sb
        E("vector", (lambda kf_: lambda e: e.tensor_copy(KVp_sb[:, :], kf_[:, :]))(kvp_fin),
          ("dv", 1), after=(f"kvp{N_CORES}",), touch=("kvp",))

        def emit_p2_gdloads(t):
            E("gpsimd",
              (lambda t_: lambda e: e.dma_start(
                  out=zloc_sb[:, :], in_=zloc_h[:, t_ * H * TILE : (t_ + 1) * H * TILE]))(t),
              ("gd", 16), after=("zloc_used2", "zloc_stashed", "zloc"), touch=("zloc2",))
            E("gpsimd",
              (lambda t_: lambda e: e.dma_start(
                  out=qf_sb[0:64, :], in_=qfst_h[:, t_ * H * TILE : (t_ + 1) * H * TILE]))(t),
              ("gd", 16), after=("qf_used2", "qf_stashed", "kvp"), touch=("qf2",))

        def emit_p2_mask(t):
            E("gpsimd",
              (lambda t_: lambda e: e.dma_start(
                  out=mask_sb[:, :].rearrange("p (o c) -> p o c", o=8),
                  in_=mask_h.rearrange("(o p) tt -> p o tt", p=128)[
                      :, :, t_ * TILE : (t_ + 1) * TILE]))(t),
              ("od", 16), after=("mask_used",), touch=("mask",))

        emit_p2_gdloads(0)
        emit_p2_mask(0)
        for t in range(NT):
            tok0 = t * TILE
            def emit_ztot(gi):
                tc, g = gi // 4, gi % 4
                zb = 5 if gi % 2 == 0 else 0
                E("tensor",
                  (lambda gi_, zb_: lambda e: e.matmul(
                      PS[0:65, zb_ * 512 : zb_ * 512 + 512], i65_sb[:, :],
                      zloc_sb[0:65, gi_ * 512 : (gi_ + 1) * 512],
                      start=True, stop=False))(gi, zb),
                  ("pe", 1), after=("zloc2", "i65", f"b{zb}_free", "kvp", "qf2"),
                  touch=(f"b{zb}z",))
                for hl in range(4):
                    h = g * 4 + hl
                    E("tensor",
                      (lambda h_, hl_, tc_, zb_: lambda e: e.matmul(
                          PS[0:65, zb_ * 512 + hl_ * 128 : zb_ * 512 + (hl_ + 1) * 128],
                          KVp_sb[:, h_ * 65 : (h_ + 1) * 65],
                          qf_sb[0:64, h_ * TILE + tc_ * 128 : h_ * TILE + (tc_ + 1) * 128],
                          start=False, stop=(hl_ == 3)))(h, hl, tc, zb),
                      ("pe", 1), touch=(f"b{zb}z",))

            def emit_zim(gi):
                zb = 5 if gi % 2 == 0 else 0
                zs = gi % 2
                zimv = zim_sb[:, zs * 512 : (zs + 1) * 512]
                E("scalar",
                  (lambda zb_, zimv_: lambda e: e.copy(zimv_, PS[0:65, zb_ * 512 : zb_ * 512 + 512]))(zb, zimv),
                  ("sc", 1), after=(f"b{zb}z", f"zim_used{zs}"),
                  touch=(f"zim{zs}", f"b{zb}_free"))

            def emit_div(gi):
                tc, g = gi // 4, gi % 4
                zs = gi % 2
                zimv = zim_sb[:, zs * 512 : (zs + 1) * 512]
                t1c = 1024
                for hl in range(4):
                    E("tensor",
                      (lambda hl_, zimv_, t1c_: lambda e: e.transpose(
                          PSB[:, t1c_ + hl_ * 66 : t1c_ + hl_ * 66 + 65],
                          zimv_[0:65, hl_ * 128 : (hl_ + 1) * 128],
                          i65_sb[:, :]))(hl, zimv, t1c),
                      ("pe", 1), after=(f"zim{zs}", "psb70_free", "i65"),
                      touch=("psb70",))
                E("vector",
                  (lambda zs_, t1c_: lambda e: e.reciprocal(
                      rden_sb[:, zs_ * 4 : (zs_ + 1) * 4],
                      PSB[:, t1c_ + 64 : t1c_ + 64 + 3 * 66 + 1 : 66]))(zs, t1c),
                  ("dv", 1), after=("psb70",), touch=(f"rden{zs}",))
                base = tc * 1024 + g * 256
                t1v = PSB[:, t1c : t1c + 264].rearrange("p (h c) -> p h c", h=4)[:, :, 0:64]
                rdv = rden_sb[:, zs * 4 : zs * 4 + 4].rearrange("p (h c) -> p h c", h=4)
                t1b, rdb = bass.broadcast_tensor_aps(t1v, rdv)
                E("vector",
                  (lambda t1b_, rdb_, base_: lambda e: e.tensor_tensor(
                      ztm_sb[:, base_ : base_ + 256].rearrange("p (h c) -> p h c", h=4),
                      t1b_, rdb_, OP.mult))(t1b, rdb, base),
                  ("dv", 1), after=(f"rden{zs}", "psb70", f"ztm_used{tc}"),
                  touch=(f"ztm{tc}", "psb70_free", f"zim_used{zs}"))

            emit_ztot(0)
            emit_ztot(1)
            for gi in range(16):
                emit_zim(gi)
                if gi + 2 < 16:
                    emit_ztot(gi + 2)
                emit_div(gi)
            P.mark("zloc_used2", ("pe",))
            P.mark("qf_used2", ("pe",))
            if t + 1 < NT:
                emit_p2_gdloads(t + 1)
            for tc in range(NTC):
                E("sync",
                  (lambda tc_, tok0_: lambda e: e.dma_start(
                      out=x_sb[:, tc_ * D : (tc_ + 1) * D],
                      in_=x_h[tok0_ + tc_ * 128 : tok0_ + (tc_ + 1) * 128, :]))(tc, tok0),
                  ("ld", 16), after=("p2x_used", "w1stage_used_0", "w2stage_used_0"),
                  touch=(f"p2x_{tc}",))

            def emit_t2(tc):
                for p in range(8):
                    E("tensor",
                      (lambda p_, tc_: lambda e: e.transpose(
                          PSB[:, p_ * 128 : (p_ + 1) * 128],
                          ztm_sb[:, tc_ * 1024 + p_ * 128 : tc_ * 1024 + (p_ + 1) * 128],
                          idb_sb[:, :]))(p, tc),
                      ("pe", 1), after=(f"ztm{tc}", "psb6_free", "idb"), touch=("psb6",))
                E("scalar",
                  (lambda tc_: lambda e: e.copy(zfm_sb[:, tc_ * 1024 : (tc_ + 1) * 1024], PSB[:, 0:1024]))(tc),
                  ("sc", 1), after=("psb6",), touch=(f"zfm{tc}", "psb6_free", f"ztm_used{tc}"))

            def emit_wo(tc):
                for half in range(2):
                    for p in range(8):
                        E("tensor",
                          (lambda half_, p_, tc_: lambda e: e.matmul(
                              PS[:, 1536 + half_ * 512 : 1536 + half_ * 512 + 512],
                              zfm_sb[:, tc_ * 1024 + p_ * 128 : tc_ * 1024 + (p_ + 1) * 128],
                              wo_sb[:, p_ * D + half_ * 512 : p_ * D + half_ * 512 + 512],
                              start=(p_ == 0), stop=(p_ == 7)))(half, p, tc),
                          ("pe", 1), after=(f"zfm{tc}", "w3", f"b{3 + half}_free"),
                          touch=(f"b{3 + half}",))
                    E("vector",
                      (lambda tc_, half_: lambda e: e.tensor_tensor(
                          x_sb[:, tc_ * D + half_ * 512 : tc_ * D + half_ * 512 + 512],
                          PS[:, 1536 + half_ * 512 : 1536 + half_ * 512 + 512],
                          x_sb[:, tc_ * D + half_ * 512 : tc_ * D + half_ * 512 + 512],
                          OP.add))(tc, half),
                      ("dv", 1), after=(f"b{3 + half}", f"p2x_{tc}"),
                      touch=(f"x1_{tc}", f"b{3 + half}_free"))

            emit_t2(0)
            for tc in range(NTC):
                if tc + 1 < NTC:
                    emit_t2(tc + 1)
                emit_wo(tc)
            for tc in range(NTC):
                pp = tc % 2
                xn2v = xn2_sb[:, pp * D : (pp + 1) * D]
                if True:
                    un = f"xn2c{pp}_used"
                    P.res.setdefault(un, {})
                    extra = dict(P.res[un])
                    for r in (f"out_used{pp}", f"outv{pp}"):
                        for k, v in P.res.get(r, {}).items():
                            if extra.get(k, 0) < v:
                                extra[k] = v
                    P.res[un] = extra
                ln_stats_and_norm(tc, xn2v, f"xn2c{pp}", f"x1_{tc}")
                for dcg in range(2):
                    for d4 in range(4):
                        dc = dcg * 4 + d4
                        E("tensor",
                          (lambda dc_, d4_, xn2v_, pp_: lambda e: e.transpose(
                              PS[:, pp_ * 512 + d4_ * 128 : pp_ * 512 + (d4_ + 1) * 128],
                              xn2v_[:, dc_ * 128 : (dc_ + 1) * 128],
                              idf_sb[:, :]))(dc, d4, xn2v, pp),
                          ("pe", 1), after=(f"xn2c{pp}", f"b{pp}_free", "idf"), touch=(f"b{pp}",))
                    E("scalar",
                      (lambda dcg_, tc_, pp_: lambda e: e.copy(
                          xn2q_sb[:, :].rearrange("p (dc c) -> p dc c", dc=8)[
                              :, dcg_ * 4 : dcg_ * 4 + 4, tc_ * 128 : (tc_ + 1) * 128],
                          bank(pp_).rearrange("p (j c) -> p j c", j=4)))(dcg, tc, pp),
                      ("sc", 1), after=(f"b{pp}",),
                      touch=("x2q_part", f"b{pp}_free"))
                P.mark(f"xn2c{pp}_used", ("pe",))
            for dc in range(8):
                P.res[f"x2q_{dc}"] = dict(P.res["x2q_part"])
            w1t3 = w1t_sb[:, :].rearrange("p (dc h) -> p dc h", dc=8)
            xn2q3 = xn2q_sb[:, :].rearrange("p (dc t) -> p dc t", dc=8)
            for hc in range(32):
                pb = hc % 3
                for dcp in range(4):
                    E("tensor",
                      (lambda hc_, dcp_, pb_: lambda e: e.matmul(
                          bank(pb_),
                          w1t3[:, 2 * dcp_ : 2 * dcp_ + 2, hc_ * 128 : (hc_ + 1) * 128],
                          xn2q3[:, 2 * dcp_ : 2 * dcp_ + 2, :],
                          start=(dcp_ == 0), stop=(dcp_ == 3),
                          perf_mode=mybir.MatmulPerfMode.DoubleRow))(hc, dcp, pb),
                      ("pe", 1),
                      after=(f"x2q_{2 * dcp}", f"x2q_{2 * dcp + 1}", f"b{pb}_free", "w1t"),
                      touch=(f"b{pb}",))
                hs = hc % 2
                E("scalar",
                  (lambda hc_, pb_, hs_: lambda e: e.activation(
                      h16_sb[:, hs_ * TILE : (hs_ + 1) * TILE], bank(pb_),
                      AF.Relu, bias=b1q_sb[:, hc_ : hc_ + 1]))(hc, pb, hs),
                  ("sc", 1), after=(f"b{pb}", f"h16_used{hs}", "b1q"),
                  touch=(f"h16{hs}", f"b{pb}_free"))
                E("vector",
                  (lambda hc_, hs_: lambda e: e.tensor_copy(
                      h8_sb[:, hc_ * TILE : (hc_ + 1) * TILE],
                      h16_sb[:, hs_ * TILE : (hs_ + 1) * TILE]))(hc, hs),
                  ("dv", 1), after=(f"h16{hs}", "w2stage_used_1"),
                  touch=(f"h8_{hc}", f"h16_used{hs}"))
            w2t3 = w2t_sb[:, :].rearrange("p (hc o) -> p hc o", hc=32)
            h83 = h8_sb[:, :].rearrange("p (hc t) -> p hc t", hc=32)
            for oc in range(8):
                pb = oc % 3
                for hcp in range(16):
                    E("tensor",
                      (lambda oc_, hcp_, pb_: lambda e: e.matmul(
                          bank(pb_),
                          w2t3[:, 2 * hcp_ : 2 * hcp_ + 2, oc_ * 128 : (oc_ + 1) * 128],
                          h83[:, 2 * hcp_ : 2 * hcp_ + 2, :],
                          start=(hcp_ == 0), stop=(hcp_ == 15),
                          perf_mode=mybir.MatmulPerfMode.DoubleRow))(oc, hcp, pb),
                      ("pe", 1),
                      after=(f"h8_{2 * hcp}", f"h8_{2 * hcp + 1}", f"b{pb}_free", "w2t"),
                      touch=(f"b{pb}",))
                ys = oc % 2
                E("scalar",
                  (lambda oc_, pb_, ys_: lambda e: e.activation(
                      y16_sb[:, ys_ * TILE : (ys_ + 1) * TILE], bank(pb_),
                      AF.Identity, bias=b2q_sb[:, oc_ : oc_ + 1]))(oc, pb, ys),
                  ("sc", 1), after=(f"b{pb}", f"y16_used{ys}", "b2q"),
                  touch=(f"y16{ys}", f"b{pb}_free"))
                E("vector",
                  (lambda oc_, ys_: lambda e: e.tensor_tensor(
                      t_sb[:, oc_ * TILE : (oc_ + 1) * TILE],
                      y16_sb[:, ys_ * TILE : (ys_ + 1) * TILE],
                      mask_sb[:, oc_ * TILE : (oc_ + 1) * TILE], OP.mult))(oc, ys),
                  ("dv", 1), after=(f"y16{ys}", "mask", f"t_used_{oc}"),
                  touch=(f"t_{oc}", f"y16_used{ys}") + (("mask_used",) if oc == 7 else ()))
            if t + 1 < NT:
                emit_p2_mask(t + 1)
            for tc in range(NTC):
                pp = tc % 2
                outv_sl = out_sb[:, pp * D : (pp + 1) * D]
                pb7 = 1024 - pp * 1024
                bw = ("psb70_free", "psb70") if pp == 0 else ("psb6_free",)
                bf = "psb70_free" if pp == 0 else "psb6_free"
                for oc in range(8):
                    E("tensor",
                      (lambda oc_, tc_, pb7_: lambda e: e.transpose(
                          PSB[:, pb7_ + oc_ * 128 : pb7_ + (oc_ + 1) * 128],
                          t_sb[:, oc_ * TILE + tc_ * 128 : oc_ * TILE + (tc_ + 1) * 128],
                          idb_sb[:, :]))(oc, tc, pb7),
                      ("pe", 1), after=(f"t_{oc}", "idb") + bw, touch=("psb7o",))
                E("vector",
                  (lambda tc_, ov_, pb7_: lambda e: e.tensor_tensor(
                      ov_[:, :], PSB[:, pb7_ : pb7_ + 1024],
                      x_sb[:, tc_ * D : (tc_ + 1) * D], OP.add))(tc, outv_sl, pb7),
                  ("dv", 1),
                  after=("psb7o", f"x1_{tc}", f"out_used{pp}",
                         f"xn2c{pp}_used", f"xn2c{pp}"),
                  touch=(f"outv{pp}", bf) + (("p2x_used",) if tc == NTC - 1 else ()))
                E("gpsimd",
                  (lambda t_, tc_, ov_: lambda e: e.dma_start(
                      out=out_h[t_ * TILE + tc_ * 128 : t_ * TILE + (tc_ + 1) * 128, :],
                      in_=ov_[:, :]))(t, tc, outv_sl),
                  ("od", 16), after=(f"outv{pp}",), touch=(f"out_used{pp}",))
            for oc in range(8):
                P.mark(f"t_used_{oc}", ("pe",))

        if max_steps is not None:
            P.steps = P.steps[:max_steps]
        block = es.enter_context(nc.Block())
        final_gd = 0
        final_od = 0
        for engine, fn, waits, isem in P.steps:
            if isem == "gd":
                final_gd += 16
            elif isem == "od":
                final_od += 16

        def inc_amount(s):
            return 16 if s in ("ld", "gd", "lw", "gw", "od") else 1

        MAX_DMA_OUT = 8

        def run_engine(name):
            def body(eng):
                waited = {}
                issued = {}
                for engine, fn, waits, isem in P.steps:
                    if engine != name:
                        continue
                    for s, v in sorted(waits.items()):
                        if waited.get(s, 0) < v:
                            eng.wait_ge(sem[s], v)
                            waited[s] = v
                    if isem in ("ld", "gd", "lw", "gw", "od"):
                        n = issued.get(isem, 0)
                        if n - waited.get(isem, 0) // 16 >= MAX_DMA_OUT:
                            need = 16 * n
                            eng.wait_ge(sem[isem], need)
                            waited[isem] = need
                        issued[isem] = n + 1
                    fn(eng).then_inc(sem[isem], inc_amount(isem))
                if name == "gpsimd":
                    eng.wait_ge(sem["gd"], final_gd)
                    eng.wait_ge(sem["od"], final_od)
            return body

        block.sync(run_engine("sync"))
        block.vector(run_engine("vector"))
        block.scalar(run_engine("scalar"))
        block.tensor(run_engine("tensor"))
        block.gpsimd(run_engine("gpsimd"))

    return nc


_NC_CACHE = {}


def _get_nc(tokens_per_core):
    if tokens_per_core not in _NC_CACHE:
        _NC_CACHE[tokens_per_core] = build_nc(tokens_per_core)
    return _NC_CACHE[tokens_per_core]


def make_in_maps(inputs, tokens_per_core, n_seq_shards=4):
    X = np.ascontiguousarray(np.asarray(inputs["L0_attn_x"], np.float32))
    B, S, _ = X.shape
    Wq = np.ascontiguousarray(np.asarray(inputs["L0_attn_Wq"], np.float32))
    Wk = np.ascontiguousarray(np.asarray(inputs["L0_attn_Wk"], np.float32))
    Wv = np.ascontiguousarray(np.asarray(inputs["L0_attn_Wv"], np.float32))
    Wo = np.ascontiguousarray(np.asarray(inputs["L0_attn_Wo"], np.float32))
    proj = np.asarray(inputs["L0_attn_proj"], np.float32)
    w1 = np.ascontiguousarray(np.asarray(inputs["L0_ffn_w1"], np.float32))
    b1 = np.asarray(inputs["L0_ffn_b1"], np.float32)
    w2 = np.ascontiguousarray(np.asarray(inputs["L0_ffn_w2"], np.float32))
    b2 = np.asarray(inputs["L0_ffn_b2"], np.float32)

    proj2 = np.zeros((128, 128), np.float32)
    proj2[0:64, 0:64] = proj
    proj2[64:128, 64:128] = proj
    b1v = np.ascontiguousarray(b1.reshape(32, 128).T)
    b2v = np.ascontiguousarray(b2.reshape(8, 128).T)
    trilm = np.ascontiguousarray(
        np.tile(np.tril(np.ones((128, 128), np.float32)).T, (1, 4))
    ).astype(ml_dtypes.bfloat16)
    idf = np.eye(128, dtype=np.float32)
    idb = np.eye(128, dtype=np.float32).astype(ml_dtypes.bfloat16)
    i65 = np.eye(65, dtype=np.float32).astype(ml_dtypes.bfloat16)

    keep = _bernoulli_keep(42, 0.8, (B, S, D))
    mask8 = (keep.astype(np.float32) * np.float32(1.25)).astype(ml_dtypes.float8_e4m3)

    in_maps = []
    for c in range(N_CORES):
        b = c // n_seq_shards
        qt = c % n_seq_shards
        xs = np.ascontiguousarray(X[b, qt * tokens_per_core : (qt + 1) * tokens_per_core])
        ms = np.ascontiguousarray(
            mask8[b, qt * tokens_per_core : (qt + 1) * tokens_per_core].T
        )
        wpref = np.zeros((128, N_CORES), np.float32)
        for j in range(N_CORES):
            if j // n_seq_shards == b and j % n_seq_shards < qt:
                wpref[:, j] = 1.0
        in_maps.append(
            dict(x=xs, wq=Wq, wk=Wk, wv=Wv, wo=Wo, proj2=proj2, w1=w1, w2=w2,
                 b1v=b1v, b2v=b2v, dmask=ms, trilm=trilm, idf=idf, idb=idb,
                 i65=i65, wpref=wpref))
    return in_maps


def kernel(**inputs):
    X = np.asarray(inputs["L0_attn_x"])
    B, S, _ = X.shape
    tokens_per_core = (B * S) // N_CORES
    nc = _get_nc(tokens_per_core)
    in_maps = make_in_maps(inputs, tokens_per_core)
    res = run_bass_kernel_spmd(nc, in_maps, core_ids=list(range(N_CORES)))
    out = np.empty((B, S, D), np.float32)
    for c in range(N_CORES):
        b = c // 4
        qt = c % 4
        out[b, qt * tokens_per_core : (qt + 1) * tokens_per_core] = res.results[c]["out"]
    return out
